# version 14
# baseline (speedup 1.0000x reference)
import math
import contextlib
import numpy as np

import concourse.bass as bass
import concourse.mybir as mybir
from concourse.bass_utils import run_bass_kernel_spmd
from concourse.alu_op_type import AluOpType

F32 = mybir.dt.float32
F32R = mybir.dt.float32r
BF16 = mybir.dt.bfloat16
AF = mybir.ActivationFunctionType
AX = mybir.AxisListType

T = 2048
D = 1024
NH = 16
HD = 64
NC = D // 128
SI = 2048
NE = 8
EI = 512
HALF = T // 2
ROPE_BASE = 10000.0
QK_SCALE = 1.0 / math.sqrt(8.0)
LN_EPS = 1e-5

ENGINES = ("tensor", "vector", "scalar", "sync", "gpsimd")


def _host_constants():
    import ml_dtypes
    c = {}
    invf32 = np.array([
        0x3f800000, 0x3f3ff911, 0x3f0ff59a, 0x3ed7e89b, 0x3ea1e89b, 0x3e72d423,
        0x3e361887, 0x3e088d77, 0x3dcccccd, 0x3d99940d, 0x3d6655c2, 0x3d2cba15,
        0x3d0186e3, 0x3cc2434f, 0x3c91ad39, 0x3c5a7bf2, 0x3c23d70a, 0x3bf5b9b0,
        0x3bb8449c, 0x3b8a2e77, 0x3b4f3e38, 0x3b1b690d, 0x3ae91528, 0x3aaec98e,
        0x3a83126f, 0x3a44948c, 0x3a136a16, 0x39dd1725, 0x39a5cb60, 0x3978a815,
        0x393a7753, 0x390bd472], dtype=np.uint32).view(np.float32)
    t32 = np.arange(T, dtype=np.float32)
    cos_l, sin_l = [], []
    for i in range(NH):
        fsc = np.float32(ROPE_BASE ** (i / NH))
        f32 = (invf32 * fsc).astype(np.float32)
        ang = (t32[:, None] * f32[None, :]).astype(np.float32)
        cos_l.append(np.cos(ang, dtype=np.float32))
        sin_l.append(np.sin(ang, dtype=np.float32))
    c["cos_t"] = np.stack(cos_l, 1).reshape(T, NH * 32).astype(np.float32)
    c["sin_t"] = np.stack(sin_l, 1).reshape(T, NH * 32).astype(np.float32)
    c["ident_bf"] = np.eye(128, dtype=ml_dtypes.bfloat16)
    c["ident_f"] = np.eye(128, dtype=np.float32)
    c["ones_f"] = np.ones((128, 1), np.float32)
    c["ones_r"] = np.ones((128, 1), np.float32)
    c["ones_row_f"] = np.ones((1, 128), np.float32)
    c["ones_row_bf"] = np.ones((1, 128), ml_dtypes.bfloat16)
    c["ones64_bf"] = np.ones((64, 1), ml_dtypes.bfloat16)
    c["ones128_bf"] = np.ones((128, 1), ml_dtypes.bfloat16)
    m = np.zeros((128, 128), np.float32)
    m[np.triu_indices(128, k=1)] = -1.0e30
    c["diag_mask"] = m
    oh = np.zeros((9, 9 * 128), np.float32)
    for e in range(9):
        oh[e, e * 128:(e + 1) * 128] = 1.0
    c["onehot9"] = oh.astype(ml_dtypes.bfloat16)
    return c


def _host_prep(inputs):
    A = np.ascontiguousarray
    x = inputs["L0_attn_x"]
    wqkv = inputs["L0_attn_Wqkv"]
    idx = np.arange(3 * D)
    for base in (0, D):
        blk = []
        for par in (0, 1):
            for h in range(NH):
                blk.extend(range(base + h * HD + par, base + (h + 1) * HD, 2))
        idx[base:base + D] = blk
    w = {
        "wqkv": A(wqkv[:, idx]),
        "wproj": A(inputs["L0_attn_Wproj"]),
        "casc": A(inputs["L0_attn_cascade_w"].reshape(1, NH - 1)),
        "wgT": A(inputs["L0_ffn_shared_gate_w"].T),
        "wuT": A(inputs["L0_ffn_shared_up_w"].T),
        "wdT": A(inputs["L0_ffn_shared_down_w"].T),
        "wguT": A(inputs["L0_ffn_gate_up_proj"].transpose(0, 2, 1)),
        "wdnT": A(inputs["L0_ffn_down_proj"].transpose(0, 2, 1)),
        "wr9": A(np.concatenate([inputs["L0_ffn_router_weight"],
                                 inputs["L0_ffn_shared_expert_gate_w"]], 0).T),
    }
    w.update(_host_constants())
    return [dict(w, xin=A(x[b].T)) for b in range(8)]


class Prog:

    def __init__(self):
        self.ev = {e: [] for e in ENGINES}
        self.ct = {"sT": 0, "sV": 0, "sS": 0, "sG": 0}
        self.eng_sem = {"tensor": "sT", "vector": "sV", "scalar": "sS",
                        "gpsimd": "sG"}
        self.dma_sems = set()
        self.nev = {e: 0 for e in ENGINES}
        self.drained_upto = {e: -1 for e in ENGINES}
        self.force_drain = True

    def add(self, eng, fn, deps=(), signal=False, dma=False, drain=False):
        waits = {}
        drainable = eng in ("vector", "scalar")
        need_drain = (bool(drain) or self.force_drain) and drainable
        for d in deps:
            if d is None:
                continue
            deng, dsem, dval, didx = d
            if deng == eng:
                if drainable and didx > self.drained_upto[eng]:
                    need_drain = True
                continue
            assert dval is not None, f"cross-engine dep unsignalled ({deng}->{eng})"
            waits[dsem] = max(waits.get(dsem, 0), dval)
        incs = []
        val = None
        if dma:
            sem = "d_" + dma
            self.dma_sems.add(sem)
            self.ct[sem] = self.ct.get(sem, 0) + 16
            val = self.ct[sem]
            incs.append((sem, 16))
        elif signal:
            sem = self.eng_sem[eng]
            self.ct[sem] += 1
            val = self.ct[sem]
            incs.append((sem, 1))
        else:
            sem = self.eng_sem.get(eng, "sT")
        idx = self.nev[eng]
        self.nev[eng] += 1
        if need_drain:
            self.drained_upto[eng] = idx - 1
        self.ev[eng].append((fn, waits, incs, need_drain))
        return (eng, sem, val, idx)

    def emit(self, block, sems):
        for eng_name in ENGINES:
            events = self.ev[eng_name]

            def run(eng, events=events, eng_name=eng_name):
                last_wait = {}
                for i, (fn, waits, incs, need_drain) in enumerate(events):
                    for sem_name in sorted(waits):
                        v = waits[sem_name]
                        if last_wait.get(sem_name, -1) < v:
                            eng.wait_ge(sems[sem_name], v)
                            last_wait[sem_name] = v
                    if need_drain:
                        eng.drain()
                    inst = fn(eng)
                    for sem_name, k in incs:
                        assert inst is not None, f"signal on None inst ({eng_name} #{i})"
                        inst.then_inc(sems[sem_name], k)

            getattr(block, eng_name)(run)


class PsumMgr:
    def __init__(self):
        self.state = {}

    def _st(self, key):
        return self.state.setdefault(key, {"w": None, "r": []})

    def write_deps(self, key):
        return list(self._st(key)["r"])

    def wrote(self, key, tok):
        s = self._st(key)
        s["w"] = tok
        s["r"] = []

    def read_deps(self, key):
        w = self._st(key)["w"]
        return [w] if w is not None else []

    def read(self, key, tok):
        self._st(key)["r"].append(tok)


def build_program(stages=("ffnA", "attn", "ffnC")):
    nc = bass.Bass("TRN2", target_bir_lowering=False, debug=False)
    P = Prog()
    PM = PsumMgr()

    dr = {}
    def din(name, shape, dt=F32):
        dr[name] = nc.dram_tensor(name, shape, dt, kind="ExternalInput").ap()
    din("xin", [D, T])
    din("wqkv", [D, 3 * D]); din("wproj", [D, D]); din("casc", [1, NH - 1])
    din("wgT", [D, SI]); din("wuT", [D, SI]); din("wdT", [SI, D])
    din("wguT", [NE, D, D]); din("wdnT", [NE, EI, D]); din("wr9", [D, 9])
    din("cos_t", [T, 512]); din("sin_t", [T, 512])
    din("ident_bf", [128, 128], BF16); din("ident_f", [128, 128])
    din("ones_f", [128, 1]); din("ones_r", [128, 1], F32R)
    din("ones_row_f", [1, 128]); din("ones_row_bf", [1, 128], BF16)
    din("ones64_bf", [64, 1], BF16); din("diag_mask", [128, 128])
    din("ones128_bf", [128, 1], BF16)
    din("onehot9", [9, 9 * 128], BF16)
    yout = nc.dram_tensor("yout", [D, T], F32, kind="ExternalOutput").ap()
    xspill = nc.dram_tensor("xspill", [D, T], F32, kind="Internal").ap()

    est = contextlib.ExitStack()
    sb = lambda n, s, dt: est.enter_context(nc.sbuf_tensor(n, s, dt))
    sems = {s: est.enter_context(nc.semaphore(s)) for s in ("sT", "sV", "sS", "sG")}

    STATS = sb("STATS", [1, 2048], F32)
    C_onesf = sb("c_onesf", [128, 1], F32)
    C_onesr = sb("c_onesr", [128, 1], F32R)
    C_orowf = sb("c_orowf", [1, 128], F32)
    C_orowbf = sb("c_orowbf", [1, 128], BF16)
    C_ones64 = sb("c_ones64", [64, 1], BF16)
    C_ones128b = sb("c_ones128b", [128, 1], BF16)
    C_identb = sb("c_identb", [128, 128], BF16)
    C_identf = sb("c_identf", [128, 128], F32)
    C_mask = sb("c_mask", [128, 128], F32)
    C_casc = sb("c_casc", [1, NH - 1], F32)
    WR9 = sb("wr9_s", [128, NC * 9], BF16)
    CWFM2 = [sb("cwfm0", [9, 1024], BF16), sb("cwfm1", [9, 1024], BF16)]
    C_oh9 = sb("c_oh9", [9, 9 * 128], BF16)
    WVROW = sb("wvrow", [1, 32], F32)
    PS = [est.enter_context(nc.psum_tensor(f"pb{i}", [128, 1024], F32)) for i in range(4)]
    xsc1 = contextlib.ExitStack()
    X = xsc1.enter_context(nc.sbuf_tensor("X1", [128, NC * T], F32))

    for nm, dst in (("ones_f", C_onesf), ("ones_row_f", C_orowf), ("diag_mask", C_mask),
                    ("casc", C_casc), ("ident_f", C_identf), ("ones_r", C_onesr),
                    ("ident_bf", C_identb), ("ones_row_bf", C_orowbf),
                    ("ones64_bf", C_ones64), ("onehot9", C_oh9),
                    ("ones128_bf", C_ones128b)):
        P.add("sync", lambda e, d=dst, s=dr[nm]: e.dma_start(out=d[:], in_=s), dma="init")
    P.add("sync", lambda e: e.dma_start(
        out=X.rearrange("p (c t) -> p c t", c=NC),
        in_=dr["xin"].rearrange("(c p) t -> p c t", p=128)), dma="init")
    t_wr9 = P.add("gpsimd", lambda e: e.dma_start(
        out=WR9.rearrange("p (c n) -> p c n", c=NC),
        in_=dr["wr9"].rearrange("(c p) n -> p c n", p=128)), dma="winit")
    T_INIT = ("sync", "d_init", P.ct["d_init"], 0)
    t_onesf = t_orowf = t_mask = t_casc = t_identf = T_INIT
    t_onesr = t_identb = t_orowbf = t_ones64 = t_oh9 = t_X = T_INIT

    t_wz = P.add("vector", lambda e: e.memset(WVROW[:, :], 0.0), signal=True)
    P.add("vector", lambda e: e.tensor_scalar(
        WVROW[:, 0:NH - 1], C_casc[:], 8.0 / T, None, AluOpType.mult), deps=[T_INIT])
    t_wv = P.add("scalar", lambda e: e.activation(
        WVROW[:, 16:16 + NH - 1], C_casc[:], AF.Abs), deps=[T_INIT, t_wz], signal=True)

    st = {"X": t_X, "Xh": {0: t_X, 1: t_X}, "Xc": {}}

    def layer_norm(XT, XNt, h0, xsq, scr, xn_w=1024, xn_off=0, extra=()):
        ht = 1024
        tok_sq_prev = [None, None]
        tok_mm1 = tok_mm2 = None
        hh = 0 if h0 < HALF else 1
        xdep_c = lambda c: st["Xc"].get((hh, c), st["X"])
        d0 = (PM.write_deps("p0") + PM.write_deps("p1")
              + [t_onesf, t_onesr] + list(extra))
        for c in range(NC):
            slot = c % 2
            xsl = XT[:, c * T + h0: c * T + h0 + ht]
            xbs = XNt[:, c * xn_w + xn_off: c * xn_w + xn_off + 1024]
            tok_sq = P.add("scalar", lambda e, i=xsl, sl=slot: e.activation(
                xsq[:, sl * ht:(sl + 1) * ht], i, AF.Square),
                deps=[xdep_c(c), tok_sq_prev[slot]] + list(extra), signal=True)
            tok_xb = P.add("scalar", lambda e, i=xsl, o=xbs: e.copy(o, i),
                           deps=[xdep_c(c)], signal=True)
            for g in range(2):
                tok_mm1 = P.add("tensor", lambda e, c=c, g=g, i=xbs: e.matmul(
                    PS[0][0:1, g * 512:(g + 1) * 512], C_ones128b[:],
                    i[:, g * 512:(g + 1) * 512], start=(c == 0), stop=(c == NC - 1)),
                    deps=[tok_xb] + (d0 if c == 0 else []), signal=(c == NC - 1))
                tok_mm2 = P.add("tensor", lambda e, c=c, g=g, sl=slot: e.matmul(
                    PS[1][0:1, g * 512:(g + 1) * 512], C_onesr[:],
                    xsq[:, sl * ht + g * 512: sl * ht + (g + 1) * 512],
                    start=(c == 0), stop=(c == NC - 1)),
                    deps=[tok_sq] + (d0 if c == 0 else []), signal=(g == 1))
            tok_sq_prev[slot] = tok_mm2
        PM.wrote("p0", tok_mm1); PM.wrote("p1", tok_mm2)
        A_row, B_row = STATS[0:1, 0:1024], STATS[0:1, 1024:2048]
        m_row = scr[0:1, 0:1024]
        dv = PM.read_deps("p0") + PM.read_deps("p1")
        P.add("vector", lambda e: e.tensor_scalar(
            m_row[:, :], PS[0][0:1, 0:1024], 1.0 / D, None, AluOpType.mult), deps=dv)
        tok_e2 = P.add("vector", lambda e: e.tensor_scalar(
            B_row[:, :], PS[1][0:1, 0:1024], 1.0 / D, LN_EPS,
            AluOpType.mult, AluOpType.add), signal=True)
        PM.read("p0", tok_e2); PM.read("p1", tok_e2)
        P.add("vector", lambda e: e.scalar_tensor_tensor(
            A_row[:, :], m_row[:, :], -1.0, m_row[:, :],
            AluOpType.mult, AluOpType.mult))
        tok_var = P.add("vector", lambda e: e.tensor_tensor(
            B_row[:, :], B_row[:, :], A_row[:, :], AluOpType.add), signal=True)
        tok_sd = P.add("scalar", lambda e: e.activation(
            B_row[:, :], B_row[:, :], AF.Sqrt), deps=[tok_var], signal=True)
        tok_A = P.add("vector", lambda e: e.reciprocal(A_row[:, :], B_row[:, :]),
                      deps=[tok_sd], signal=True)
        tok_B = P.add("vector", lambda e: e.scalar_tensor_tensor(
            B_row[:, :], m_row[:, :], -1.0, A_row[:, :],
            AluOpType.mult, AluOpType.mult), signal=True)
        dA = PM.write_deps("p2") + [tok_A, t_orowf]
        dB = PM.write_deps("p3") + [tok_B]
        tok_rA = tok_rB = None
        for g in range(2):
            tok_rA = P.add("tensor", lambda e, g=g: e.matmul(
                PS[2][:, g * 512:(g + 1) * 512], C_orowf[:],
                A_row[:, g * 512:(g + 1) * 512], start=True, stop=True),
                deps=dA if g == 0 else [], signal=(g == 1))
            tok_rB = P.add("tensor", lambda e, g=g: e.matmul(
                PS[3][:, g * 512:(g + 1) * 512], C_orowf[:],
                B_row[:, g * 512:(g + 1) * 512], start=True, stop=True),
                deps=dB if g == 0 else [], signal=(g == 1))
        PM.wrote("p2", tok_rA); PM.wrote("p3", tok_rB)
        dAp = PM.read_deps("p2") + PM.read_deps("p3")
        tok_xn = None
        for c in range(NC):
            xsl = XT[:, c * T + h0: c * T + h0 + 1024]
            nsl = XNt[:, c * xn_w + xn_off: c * xn_w + xn_off + 1024]
            P.add("vector", lambda e, i=xsl: e.tensor_tensor(
                scr[:, 0:1024], i, PS[2][:, 0:1024], AluOpType.mult),
                deps=dAp if c == 0 else [])
            tok_xn = P.add("vector", lambda e, o=nsl: e.tensor_tensor(
                o, scr[:, 0:1024], PS[3][:, 0:1024], AluOpType.add),
                signal=(c == NC - 1))
        PM.read("p2", tok_xn); PM.read("p3", tok_xn)
        return tok_xn

    def router_half(XNt, h0, scr, tok_xn, CWFM):
        BIG = 1.0e30
        dW = PM.write_deps("p0") + [tok_xn, t_wr9]
        tok_lg = None
        for tt in range(8):
            for c in range(NC):
                tok_lg = P.add("tensor", lambda e, tt=tt, c=c: e.matmul(
                    PS[0][:, tt * 16: tt * 16 + 9],
                    XNt[:, c * 1024 + tt * 128: c * 1024 + (tt + 1) * 128],
                    WR9[:, c * 9:(c + 1) * 9], start=(c == 0), stop=(c == NC - 1)),
                    deps=dW if (tt == 0 and c == 0) else [], signal=(c == NC - 1))
        PM.wrote("p0", tok_lg)
        dl = PM.read_deps("p0")
        BIG = 1.0e30
        lg3 = PS[0].rearrange("p (t k) -> p t k", k=16)[:, 0:8, 0:8]
        sg3 = PS[0].rearrange("p (t k) -> p t k", k=16)[:, 0:8, 8:9]
        r3_ = lambda a, k: a.rearrange("p (t k) -> p t k", k=k)
        m1 = scr[:, 0:8]; m2 = scr[:, 8:16]; dd = scr[:, 16:24]
        sd = scr[:, 24:32]; sdb = scr[:, 32:40]; sgc = scr[:, 40:48]
        eq1 = scr[:, 64:128]; l2 = scr[:, 128:192]; eq2 = scr[:, 192:256]
        t1 = scr[:, 256:320]
        csg = scr[:, 320:392]
        P.add("vector", lambda e: e.tensor_reduce(
            r3_(m1, 1), lg3, AX.X, AluOpType.max), deps=dl)
        P.add("vector", lambda e: e.tensor_tensor(
            r3_(eq1, 8), lg3, r3_(m1, 1).broadcast_to((128, 8, 8)), AluOpType.is_equal))
        P.add("vector", lambda e: e.scalar_tensor_tensor(
            r3_(l2, 8), r3_(eq1, 8), -BIG, lg3, AluOpType.mult, AluOpType.add))
        P.add("vector", lambda e: e.tensor_reduce(
            r3_(m2, 1), r3_(l2, 8), AX.X, AluOpType.max))
        P.add("vector", lambda e: e.tensor_tensor(
            r3_(eq2, 8), r3_(l2, 8), r3_(m2, 1).broadcast_to((128, 8, 8)), AluOpType.is_equal))
        tok_d = P.add("vector", lambda e: e.tensor_tensor(
            dd, m1, m2, AluOpType.subtract), signal=True)
        P.add("scalar", lambda e: e.activation(sd, dd, AF.Sigmoid), deps=[tok_d])
        tok_sg = P.add("scalar", lambda e: e.activation(
            sgc, sg3.rearrange("p t k -> p (t k)") if False else sgc, AF.Sigmoid)
            if False else e.activation(sgc, PS[0].rearrange("p (t k) -> p t k", k=16)[:, 0:8, 8:9].rearrange("p t k -> p (t k)"), AF.Sigmoid), signal=True)
        P.add("vector", lambda e: e.tensor_scalar(
            sd, sd, 0.5, None, AluOpType.mult), deps=[tok_sg])
        P.add("vector", lambda e: e.tensor_scalar(
            sdb, sd, -1.0, 0.5, AluOpType.mult, AluOpType.add))
        P.add("vector", lambda e: e.tensor_scalar(
            r3_(csg, 9)[:, :, 8:9].rearrange("p t k -> p (t k)"), sgc, 0.5, None, AluOpType.mult))
        P.add("vector", lambda e: e.tensor_tensor(
            r3_(csg, 9)[:, :, 0:8], r3_(eq1, 8),
            r3_(sd, 1).broadcast_to((128, 8, 8)), AluOpType.mult))
        P.add("vector", lambda e: e.tensor_tensor(
            r3_(t1, 8), r3_(eq2, 8), r3_(sdb, 1).broadcast_to((128, 8, 8)), AluOpType.mult))
        tok_cw = P.add("vector", lambda e: e.tensor_tensor(
            r3_(csg, 9)[:, :, 0:8], r3_(csg, 9)[:, :, 0:8], r3_(t1, 8),
            AluOpType.add), signal=True)
        PM.read("p0", tok_cw); PM.read("p0", tok_sg)
        tok_cw_tiles = [tok_cw] * 8
        dtr = PM.write_deps("p1") + [t_identf]
        tok_tr = None
        for tt in range(8):
            csg_t = scr[:, 320 + 9 * tt: 320 + 9 * tt + 9]
            tok_tr = P.add("tensor", lambda e, tt=tt, csg=csg_t: e.matmul(
                PS[1][0:9, tt * 128:(tt + 1) * 128], csg, C_identf[:],
                start=True, stop=True),
                deps=[tok_cw_tiles[tt], tok_sg] + (dtr if tt == 0 else []), signal=(tt == 7))
        PM.wrote("p1", tok_tr)
        dtv = PM.read_deps("p1")
        tok_cwfm = P.add("vector", lambda e: e.tensor_copy(CWFM[:, :], PS[1][0:9, 0:1024]),
                         deps=dtv, signal=True)
        PM.read("p1", tok_cwfm)
        return tok_cwfm

    def ffn_call(XT, XNt, warena, s_sb, acts, xsq, scr, silu_sb, sgb, cwb, guard=()):
        R = [warena[:, i * 2048:(i + 1) * 2048] for i in range(4)]
        guard = list(guard)
        reg_free = [guard, guard, guard, guard]

        last_tok = {}

        def evac_gu(rep_sb, tok_rep, kind_idx, out_sl, sil_slot, sil_cons, pg, pu):
            ssl = silu_sb[:, sil_slot * 2048:(sil_slot + 1) * 2048]
            kg, ku = f"p{pg}", f"p{pu}"
            tok_sil = P.add("scalar", lambda e, ssl=ssl, pg=pg: e.activation(
                ssl[:, 0:1024], PS[pg][:, 0:1024], AF.Silu),
                deps=PM.read_deps(kg) + [sil_cons[sil_slot]], signal=True)
            PM.read(kg, tok_sil)
            tok_m1 = P.add("vector", lambda e, ssl=ssl, pu=pu: e.tensor_tensor(
                ssl[:, 1024:2048], ssl[:, 0:1024], PS[pu][:, 0:1024],
                AluOpType.mult), deps=PM.read_deps(ku) + [tok_sil], signal=True)
            PM.read(ku, tok_m1)
            tok_o = P.add("vector", lambda e, ssl=ssl, o=out_sl, r=rep_sb: e.tensor_tensor(
                o, ssl[:, 1024:2048], r[:, 0:1024], AluOpType.mult),
                deps=[tok_rep], signal=True)
            sil_cons[sil_slot] = tok_o
            last_tok[kind_idx] = tok_o
            return tok_o

        def ln_router(half, extra=()):
            h0 = half * HALF
            tok_xn = layer_norm(XT, XNt, h0, xsq, scr, extra=extra)
            tok_cwfm = router_half(XNt, h0, scr, tok_xn, CWFM2[half])
            dsg = PM.write_deps("p3") + [tok_cwfm, t_oh9]
            tok_sgr = None
            for g in range(2):
                tok_sgr = P.add("tensor", lambda e, g=g, half=half: e.matmul(
                    PS[3][:, g * 512:(g + 1) * 512], C_oh9[:, 8 * 128:9 * 128],
                    CWFM2[half][:, g * 512:(g + 1) * 512], start=True, stop=True),
                    deps=dsg if g == 0 else [], signal=(g == 1))
            PM.wrote("p3", tok_sgr)
            tok_sgb = P.add("vector", lambda e, half=half: e.tensor_copy(
                sgb[:, half * 1024:(half + 1) * 1024], PS[3][:, 0:1024]),
                deps=PM.read_deps("p3"), signal=True)
            PM.read("p3", tok_sgb)
            return tok_xn, tok_sgb, tok_cwfm

        pre = {0: ln_router(0, extra=guard)}
        for half in range(2):
            h0 = half * HALF
            tok_xn, tok_sgb, tok_cwfm = pre[half]
            sgb_h = sgb[:, half * 1024:(half + 1) * 1024]
            XN_OFF = 0
            sil_cons = [None, None]
            for blk in range(8):
                ra, rb = (0, 1) if blk % 2 == 0 else (2, 3)
                tok_wg = P.add("gpsimd", lambda e, blk=blk, d=R[ra]: e.dma_start(
                    out=d.rearrange("p (c n) -> p c n", c=NC),
                    in_=dr["wgT"].rearrange("(c p) s -> p c s", p=128)
                    [:, :, blk * 256:(blk + 1) * 256]),
                    deps=reg_free[ra], dma=f"w{ra}")
                tok_wu = P.add("gpsimd", lambda e, blk=blk, d=R[rb]: e.dma_start(
                    out=d.rearrange("p (c n) -> p c n", c=NC),
                    in_=dr["wuT"].rearrange("(c p) s -> p c s", p=128)
                    [:, :, blk * 256:(blk + 1) * 256]),
                    deps=reg_free[rb], dma=f"w{rb}")
                tok_g = tok_u = None
                for sci in range(2):
                    sc = blk * 2 + sci
                    pg, pu = (0, 1) if sc % 2 == 0 else (2, 3)
                    dg = PM.write_deps(f"p{pg}") + [tok_wg, tok_xn]
                    du = PM.write_deps(f"p{pu}") + [tok_wu]
                    for tc in range(2):
                        for c in range(NC):
                            xsl = XNt[:, c * 1024 + tc * 512: c * 1024 + (tc + 1) * 512]
                            tok_g = P.add("tensor", lambda e, c=c, tc=tc, sci=sci, d=R[ra], x=xsl, pg=pg: e.matmul(
                                PS[pg][:, tc * 512:(tc + 1) * 512],
                                d[:, c * 256 + sci * 128: c * 256 + (sci + 1) * 128], x,
                                start=(c == 0), stop=(c == NC - 1)),
                                deps=dg if (tc == 0 and c == 0) else [],
                                signal=(tc == 1 and c == NC - 1))
                            tok_u = P.add("tensor", lambda e, c=c, tc=tc, sci=sci, d=R[rb], x=xsl, pu=pu: e.matmul(
                                PS[pu][:, tc * 512:(tc + 1) * 512],
                                d[:, c * 256 + sci * 128: c * 256 + (sci + 1) * 128], x,
                                start=(c == 0), stop=(c == NC - 1)),
                                deps=du if (tc == 0 and c == 0) else [],
                                signal=(tc == 1 and c == NC - 1))
                    PM.wrote(f"p{pg}", tok_g); PM.wrote(f"p{pu}", tok_u)
                    evac_gu(sgb_h, tok_sgb, 2, s_sb[:, sc * 1024:(sc + 1) * 1024],
                            sc % 2, sil_cons, pg, pu)
                reg_free[ra] = [tok_g]
                reg_free[rb] = [tok_u]

            for grp in range(2):
                for ei in range(4):
                    e_id = grp * 4 + ei
                    dcw = PM.write_deps("p3") + [tok_cwfm, t_oh9]
                    tok_cwr = None
                    for g in range(2):
                        tok_cwr = P.add("tensor", lambda e, g=g, e_id=e_id, half=half: e.matmul(
                            PS[3][:, g * 512:(g + 1) * 512], C_oh9[:, e_id * 128:(e_id + 1) * 128],
                            CWFM2[half][:, g * 512:(g + 1) * 512],
                            start=True, stop=True),
                            deps=dcw if g == 0 else [], signal=(g == 1))
                    PM.wrote("p3", tok_cwr)
                    tok_cwb = P.add("vector", lambda e: e.tensor_copy(cwb[:, :], PS[3][:, 0:1024]),
                                    deps=PM.read_deps("p3"), signal=True)
                    PM.read("p3", tok_cwb)
                    for hf in range(2):
                        u = (ei * 2 + hf) % 2
                        ra, rb = (0, 1) if u == 0 else (2, 3)
                        tok_wg = P.add("gpsimd", lambda e, e_id=e_id, hf=hf, d=R[ra]: e.dma_start(
                            out=d.rearrange("p (c n) -> p c n", c=NC),
                            in_=dr["wguT"].rearrange("E (c p) n -> p (E c) n", p=128)
                            [:, e_id * NC:(e_id + 1) * NC, hf * 256:(hf + 1) * 256]),
                            deps=reg_free[ra], dma=f"w{ra}")
                        tok_wu = P.add("gpsimd", lambda e, e_id=e_id, hf=hf, d=R[rb]: e.dma_start(
                            out=d.rearrange("p (c n) -> p c n", c=NC),
                            in_=dr["wguT"].rearrange("E (c p) n -> p (E c) n", p=128)
                            [:, e_id * NC:(e_id + 1) * NC, 512 + hf * 256: 512 + (hf + 1) * 256]),
                            deps=reg_free[rb], dma=f"w{rb}")
                        tok_gt = tok_up = None
                        for pri in range(2):
                            pr = hf * 2 + pri
                            k = ei * 4 + pr
                            pg, pu = (0, 1) if k % 2 == 0 else (2, 3)
                            dgt = PM.write_deps(f"p{pg}") + [tok_wg, tok_xn]
                            dup = PM.write_deps(f"p{pu}") + [tok_wu]
                            for tc in range(2):
                                for c in range(NC):
                                    xsl = XNt[:, c * 1024 + tc * 512: c * 1024 + (tc + 1) * 512]
                                    tok_gt = P.add("tensor", lambda e, c=c, tc=tc, pri=pri, d=R[ra], x=xsl, pg=pg: e.matmul(
                                        PS[pg][:, tc * 512:(tc + 1) * 512],
                                        d[:, c * 256 + pri * 128: c * 256 + (pri + 1) * 128], x,
                                        start=(c == 0), stop=(c == NC - 1)),
                                        deps=dgt if (tc == 0 and c == 0) else [],
                                        signal=(tc == 1 and c == NC - 1))
                                    tok_up = P.add("tensor", lambda e, c=c, tc=tc, pri=pri, d=R[rb], x=xsl, pu=pu: e.matmul(
                                        PS[pu][:, tc * 512:(tc + 1) * 512],
                                        d[:, c * 256 + pri * 128: c * 256 + (pri + 1) * 128], x,
                                        start=(c == 0), stop=(c == NC - 1)),
                                        deps=dup if (tc == 0 and c == 0) else [],
                                        signal=(tc == 1 and c == NC - 1))
                            PM.wrote(f"p{pg}", tok_gt); PM.wrote(f"p{pu}", tok_up)
                            evac_gu(cwb, tok_cwb, 3, acts[:, k * 1024:(k + 1) * 1024],
                                    k % 2, sil_cons, pg, pu)
                        reg_free[ra] = [tok_gt]
                        reg_free[rb] = [tok_up]

                if grp == 1 and half == 0:
                    pre[1] = ln_router(1)
                for hb in range(NC):
                    ra, rb = (0, 1) if hb % 2 == 0 else (2, 3)
                    toks_w = []
                    if grp == 0:
                        toks_w.append(P.add("gpsimd", lambda e, hb=hb, d=R[ra]: e.dma_start(
                            out=d.rearrange("p (k n) -> p k n", k=16),
                            in_=dr["wdT"].rearrange("(k p) n -> p k n", p=128)
                            [:, :, hb * 128:(hb + 1) * 128]),
                            deps=reg_free[ra], dma=f"w{ra}"))
                    toks_w.append(P.add("gpsimd", lambda e, hb=hb, grp=grp, d=R[rb]: e.dma_start(
                        out=d.rearrange("p (k n) -> p k n", k=16),
                        in_=dr["wdnT"].rearrange("E (c p) n -> p (E c) n", p=128)
                        [:, grp * 16:(grp + 1) * 16, hb * 128:(hb + 1) * 128]),
                        deps=reg_free[rb], dma=f"w{rb}"))
                    pidx = hb % 4
                    pk = f"p{pidx}"
                    steps = ([("s", k) for k in range(16)] if grp == 0 else []) \
                        + [("a", k) for k in range(16)]
                    n_st = len(steps)
                    dmm = (PM.write_deps(pk) + toks_w + [tok_xn]
                           + [last_tok.get(2), last_tok.get(3)])
                    tok_d = None
                    for si, (kind, k) in enumerate(steps):
                        lhs_reg = R[ra] if kind == "s" else R[rb]
                        rhs_t = s_sb if kind == "s" else acts
                        for tc in range(2):
                            tok_d = P.add("tensor", lambda e, lhs_reg=lhs_reg, rhs_t=rhs_t,
                                          k=k, tc=tc, si=si, pidx=pidx, n_st=n_st: e.matmul(
                                PS[pidx][:, tc * 512:(tc + 1) * 512],
                                lhs_reg[:, k * 128:(k + 1) * 128],
                                rhs_t[:, k * 1024 + tc * 512: k * 1024 + (tc + 1) * 512],
                                start=(si == 0), stop=(si == n_st - 1)),
                                deps=dmm if si == 0 else [],
                                signal=(si == n_st - 1 and tc == 1))
                    PM.wrote(pk, tok_d)
                    if grp == 0:
                        reg_free[ra] = [tok_d]
                    reg_free[rb] = [tok_d]
                    dev = PM.read_deps(pk)
                    tok_ev = P.add("vector", lambda e, hb=hb, pidx=pidx, h0=h0, XT=XT: e.tensor_tensor(
                        XT[:, hb * T + h0: hb * T + h0 + 1024],
                        XT[:, hb * T + h0: hb * T + h0 + 1024],
                        PS[pidx][:, 0:1024], AluOpType.add), deps=dev, signal=True)
                    PM.read(pk, tok_ev)
                    st["X"] = tok_ev
                    st["Xh"][half] = tok_ev
                    st["Xc"][(half, hb)] = tok_ev


    def attention(guard):
        rsc_cat = contextlib.ExitStack()
        CAT = rsc_cat.enter_context(nc.sbuf_tensor("cat", [128, NC * T], BF16, side="right"))
        rsc_qkvm = contextlib.ExitStack()
        rbq = lambda n, s, dt: rsc_qkvm.enter_context(nc.sbuf_tensor(n, s, dt, side="right"))
        XNA = rbq("xna", [128, NC * T], BF16)
        rsc_ln = contextlib.ExitStack()
        xsq_a = rsc_ln.enter_context(nc.sbuf_tensor("xsq_a", [128, 2048], F32R, side="right"))
        scr_a = rsc_ln.enter_context(nc.sbuf_tensor("scr_a", [128, 1024], F32, side="right"))
        tok_ln = None
        for th in range(2):
            tok_ln = layer_norm(X, XNA, th * 1024, xsq_a, scr_a, xn_w=T,
                                xn_off=th * 1024, extra=guard)
        tok_spill = None
        for c in range(NC):
            tok_spill = P.add("sync", lambda e, c=c: e.dma_start(
                out=xspill[c * 128:(c + 1) * 128, :],
                in_=X[:, c * T:(c + 1) * T]),
                deps=[st["Xc"].get((0, c), st["X"]), st["Xc"].get((1, c), st["X"]),
                      tok_ln], dma="spill")
        rsc_ln.close()
        xsc1.close()

        lsc_qkv = contextlib.ExitStack()
        lb = lambda n, s, dt: lsc_qkv.enter_context(nc.sbuf_tensor(n, s, dt))
        QR = lb("qr", [128, 16 * 1024], BF16)
        KR = lb("kr", [128, 16 * 1024], BF16)
        VT = lb("vt", [128, 16 * 1024], BF16)
        WQB = rbq("wqb", [128, NC * 1024], BF16)
        TCOS = [rbq(f"tcos{i}", [128, 512], F32) for i in range(2)]
        TSIN = [rbq(f"tsin{i}", [128, 512], F32) for i in range(2)]
        RSC1 = rbq("rsc1", [128, 512], F32)
        RSC2 = rbq("rsc2", [128, 512], F32)

        def r3(ap):
            return ap.rearrange("p (h j) -> p h j", h=NH)

        wq_cons = [tok_ln]
        tab_cons = [None, None]
        qraw_cons = [None, None]
        tok_qkv_last = tok_ln
        VTS = VT[:, 0:NC * 1024]
        tok_wQ = P.add("gpsimd", lambda e: e.dma_start(
            out=WQB.rearrange("p (c n) -> p c n", c=NC),
            in_=dr["wqkv"].rearrange("(c p) n -> p c n", p=128)[:, :, 0:1024]),
            deps=wq_cons, dma="wqkv")
        tok_wK = P.add("gpsimd", lambda e: e.dma_start(
            out=VTS.rearrange("p (c n) -> p c n", c=NC),
            in_=dr["wqkv"].rearrange("(c p) n -> p c n", p=128)[:, :, 1024:2048]),
            deps=wq_cons, dma="wqkv2")
        VWS = CAT[:, 0:NC * 1024]
        tok_wV = P.add("gpsimd", lambda e: e.dma_start(
            out=VWS.rearrange("p (c n) -> p c n", c=NC),
            in_=dr["wqkv"].rearrange("(c p) n -> p c n", p=128)[:, :, 2048:3072]),
            deps=wq_cons, dma="wqkv3")
        P.force_drain = False
        for wh in range(3):
            WSRC = (WQB, VTS, VWS)[wh]
            tok_w = (tok_wQ, tok_wK, tok_wV)[wh]
            tok_mm_last = None
            for tt in range(16):
                pt = (wh * 16 + tt) % 2
                if wh < 2:
                    tok_tc = P.add("sync", lambda e, tt=tt, pt=pt: e.dma_start(
                        out=TCOS[pt][:], in_=dr["cos_t"][tt * 128:(tt + 1) * 128, :]),
                        deps=[tab_cons[pt]], dma=f"tcos{pt}")
                    tok_ts = P.add("sync", lambda e, tt=tt, pt=pt: e.dma_start(
                        out=TSIN[pt][:], in_=dr["sin_t"][tt * 128:(tt + 1) * 128, :]),
                        deps=[tab_cons[pt]], dma=f"tsin{pt}")
                par = tt % 2
                kp = f"p{par}a"
                dmm = (PM.write_deps(kp) + PM.write_deps(f"p{par}b")
                       + [tok_w, tok_ln, tok_spill])
                tok_mm = None
                for g in range(2):
                    for c in range(NC):
                        tok_mm = P.add("tensor", lambda e, tt=tt, g=g, c=c, W=WSRC, par=par: e.matmul(
                            PS[par][:, g * 512:(g + 1) * 512],
                            XNA[:, c * T + tt * 128: c * T + (tt + 1) * 128],
                            W[:, c * 1024 + g * 512: c * 1024 + (g + 1) * 512],
                            start=(c == 0), stop=(c == NC - 1)),
                            deps=dmm if (g == 0 and c == 0) else [],
                            signal=(g == 1 and c == NC - 1))
                PM.wrote(kp, tok_mm); PM.wrote(f"p{par}b", tok_mm)
                tok_mm_last = tok_mm
                dre = PM.read_deps(kp)
                if wh == 2:
                    tok_ev = P.add("scalar", lambda e, tt=tt, par=par: e.copy(
                        VT[:, tt * 1024:(tt + 1) * 1024], PS[par][:, 0:1024]),
                        deps=dre, signal=True)
                    PM.read(kp, tok_ev); PM.read(f"p{par}b", tok_ev)
                    tok_qkv_last = tok_ev
                else:
                    DST = QR if wh == 0 else KR
                    d1 = r3(DST[:, tt * 1024:(tt + 1) * 1024])[:, :, 0:32]
                    d2 = r3(DST[:, tt * 1024:(tt + 1) * 1024])[:, :, 32:64]
                    P.add("vector", lambda e, pt=pt, par=par: e.tensor_tensor(
                        RSC1[:, :], PS[par][:, 0:512], TCOS[pt][:, :], AluOpType.mult),
                        deps=dre + [tok_tc], drain=True)
                    P.add("vector", lambda e, pt=pt, par=par: e.tensor_tensor(
                        RSC2[:, :], PS[par][:, 512:1024], TSIN[pt][:, :], AluOpType.mult),
                        deps=[tok_ts])
                    P.add("vector", lambda e, o=d1, pt=pt: e.tensor_tensor(
                        o, r3(RSC1[:, :]), r3(RSC2[:, :]), AluOpType.subtract), drain=True)
                    P.add("vector", lambda e, pt=pt, par=par: e.tensor_tensor(
                        RSC1[:, :], PS[par][:, 0:512], TSIN[pt][:, :], AluOpType.mult),
                        drain=True)
                    P.add("vector", lambda e, pt=pt, par=par: e.tensor_tensor(
                        RSC2[:, :], PS[par][:, 512:1024], TCOS[pt][:, :], AluOpType.mult))
                    tok_rp = P.add("vector", lambda e, o=d2: e.tensor_tensor(
                        o, r3(RSC1[:, :]), r3(RSC2[:, :]), AluOpType.add),
                        signal=True, drain=True)
                    PM.read(kp, tok_rp); PM.read(f"p{par}b", tok_rp)
                    tab_cons[pt] = tok_rp
                    tok_qkv_last = tok_rp
        P.force_drain = True
        rsc_qkvm.close()
        for k in ("p0a", "p0b", "p1a", "p1b", "p2a", "p2b", "p3a", "p3b"):
            PM.read(k, tok_qkv_last)

        lsc_hd = contextlib.ExitStack()
        hb_ = lambda n, s, dt: lsc_hd.enter_context(nc.sbuf_tensor(n, s, dt))
        QEXT = [hb_(f"qext{i}", [65, T], BF16) for i in range(2)]
        KEXT = [hb_(f"kext{i}", [65, T], BF16) for i in range(2)]
        SQQ = hb_("sqq", [64, T], BF16)
        SQK = hb_("sqk", [64, T], BF16)
        ATT_E = [hb_(f"atte{i}", [128, T], BF16) for i in range(3)]
        ATT_T = hb_("attt", [128, 1024], BF16)
        DBUF = [hb_(f"dbuf{i}", [128, 128], BF16) for i in range(3)]
        MNEG = hb_("mneg", [128, 16], F32)
        SROW = hb_("srow", [128, 16], F32)
        RINVB = hb_("rinvb", [128, 4], BF16)
        CS_SB = hb_("cs_sb", [128, 16], BF16)
        KMX = hb_("kmx", [64, 1], BF16)
        S0SB = hb_("s0sb", [128, 1], F32)

        tok_q1 = [P.add("vector", lambda e, i=i: e.memset(QEXT[i][64:65, :], 1.0),
                        signal=True) for i in range(2)]
        ext_cons = [[tok_qkv_last, tok_q1[0], tok_q1[1]], [tok_qkv_last, tok_q1[0], tok_q1[1]]]
        slot_cons = [None, None, None]
        attt_cons = [None, None]
        cs_cons = [None]
        tok_bias = {}
        tok_mneg = {}
        tok_ext = {}
        tok_sq_cons = [None]

        def prepare_head(h):
            b = h % 2
            tok_tr_ev = []
            for rnd in range(2):
                dtr = (PM.write_deps("p1a") + PM.write_deps("p1b")
                       + ext_cons[b] + [tok_qkv_last])
                tok_tr = None
                for tti in range(8):
                    tt = rnd * 8 + tti
                    P.add("tensor", lambda e, tt=tt, tti=tti, h=h: e.matmul(
                        PS[1][0:64, tti * 128:(tti + 1) * 128],
                        QR[:, tt * 1024 + h * 64: tt * 1024 + h * 64 + 64],
                        C_identb[:], start=True, stop=True),
                        deps=dtr if tti == 0 else [])
                    tok_tr = P.add("tensor", lambda e, tt=tt, tti=tti, h=h: e.matmul(
                        PS[1][64:128, tti * 128:(tti + 1) * 128],
                        KR[:, tt * 1024 + h * 64: tt * 1024 + h * 64 + 64],
                        C_identb[:], start=True, stop=True, tile_position=(0, 64)),
                        deps=[], signal=(tti == 7))
                PM.wrote("p1a", tok_tr); PM.wrote("p1b", tok_tr)
                dtv = PM.read_deps("p1a") + PM.read_deps("p1b")
                tok_eq = P.add("scalar", lambda e, rnd=rnd, b=b: e.copy(
                    QEXT[b][0:64, rnd * 1024:(rnd + 1) * 1024], PS[1][0:64, 0:1024]),
                    deps=dtv, signal=True)
                tok_ek = P.add("vector", lambda e, rnd=rnd, b=b: e.tensor_copy(
                    KEXT[b][0:64, rnd * 1024:(rnd + 1) * 1024], PS[1][64:128, 0:1024]),
                    deps=dtv, signal=True)
                PM.read("p1a", tok_eq); PM.read("p1a", tok_ek)
                PM.read("p1b", tok_eq); PM.read("p1b", tok_ek)
                tok_tr_ev += [tok_eq, tok_ek]
            tok_sq = P.add("scalar", lambda e, b=b: e.activation(
                SQQ[:, :], QEXT[b][0:64, :], AF.Square),
                deps=tok_tr_ev + [tok_sq_cons[0]], signal=True)
            tok_sk = P.add("vector", lambda e, b=b: e.tensor_tensor(
                SQK[:, :], KEXT[b][0:64, :], KEXT[b][0:64, :], AluOpType.mult),
                deps=tok_tr_ev + [tok_sq_cons[0]], signal=True)
            dq2 = PM.write_deps("p3b") + [tok_sq, T_INIT]
            tok_q2 = None
            for qt in range(16):
                tok_q2 = P.add("tensor", lambda e, qt=qt: e.matmul(
                    PS[3][:, 768 + qt:769 + qt], SQQ[:, qt * 128:(qt + 1) * 128],
                    C_ones64[:], start=True, stop=True),
                    deps=dq2 if qt == 0 else [], signal=(qt == 15))
            tok_km = P.add("vector", lambda e: e.tensor_reduce(
                KMX[:, :], SQK[:, :], AX.X, AluOpType.max), deps=[tok_sk], signal=True)
            tok_kb = P.add("tensor", lambda e: e.matmul(
                PS[3][0:1, 784:785], C_ones64[:], KMX[:, :], start=True, stop=True),
                deps=[tok_km], signal=True)
            wa = WVROW[0:1, 16 + h - 1:16 + h] if h > 0 else WVROW[0:1, 31:32]
            tok_s0 = P.add("vector", lambda e, wa=wa: e.tensor_scalar(
                STATS[0:1, 0:1], PS[3][0:1, 784:785], 0.0625, wa,
                AluOpType.mult, AluOpType.add), deps=[tok_kb, t_wv], signal=True)
            tok_s0r = P.add("tensor", lambda e: e.matmul(
                PS[3][:, 785:786], C_orowf[:], STATS[0:1, 0:1], start=True, stop=True),
                deps=[tok_s0, tok_q2], signal=True)
            PM.wrote("p3b", tok_s0r)
            d3 = PM.read_deps("p3b")
            tok_s0sb = P.add("vector", lambda e: e.tensor_copy(
                S0SB[:, :], PS[3][:, 785:786]), deps=d3, signal=True)
            tok_mn = P.add("vector", lambda e: e.tensor_scalar(
                MNEG[:, :], PS[3][:, 768:784], -0.0625, S0SB[:, 0:1],
                AluOpType.mult, AluOpType.subtract), deps=[tok_s0sb], signal=True)
            PM.read("p3b", tok_mn)
            tok_mneg[h] = tok_mn
            tok_ext[h] = tok_tr_ev
            tok_sq_cons[0] = tok_q2
            if h == 0:
                tok_bias[0] = P.add("vector", lambda e: e.memset(
                    KEXT[0][64:65, :], 0.0), signal=True)

        def head_qt_loop(h, inject=None):
            b = h % 2
            cons = []

            def emit_scores(qt):
                par = qt % 2
                s3 = qt % 3
                kw = (qt + 1) * 128
                toks_exp = []
                for bi in range((kw + 1023) // 1024):
                    c0, c1 = bi * 1024, min(kw, (bi + 1) * 1024)
                    ka, kb = f"p{par}a", f"p{par}b"
                    dsc = (PM.write_deps(ka) + PM.write_deps(kb)
                           + [tok_mneg[h], tok_bias[h]]
                           + (tok_ext[h] if qt <= 1 and bi == 0 else []))
                    glist = list(range((c1 - c0 + 511) // 512))
                    if c1 == kw and len(glist) > 1:
                        glist = glist[::-1]
                    tok_sc = tok_diag = None
                    for gi, g in enumerate(glist):
                        g0 = c0 + g * 512
                        g1 = min(c1, g0 + 512)
                        tok_sc = P.add("tensor", lambda e, qt=qt, g0=g0, g1=g1, c0=c0, b=b, par=par: e.matmul(
                            PS[par][:, g0 - c0:g1 - c0],
                            QEXT[b][:, qt * 128:(qt + 1) * 128],
                            KEXT[b][:, g0:g1], start=True, stop=True),
                            deps=dsc if gi == 0 else [],
                            signal=(gi == 0 or gi == len(glist) - 1))
                        if gi == 0:
                            tok_diag = tok_sc
                    PM.wrote(ka, tok_sc); PM.wrote(kb, tok_sc)
                    toks = [tok_sc]
                    if c1 == kw:
                        dcol = (kw - 128) - c0
                        tok_mask = P.add("vector", lambda e, dcol=dcol, par=par: e.tensor_tensor(
                            PS[par][:, dcol:dcol + 128], PS[par][:, dcol:dcol + 128],
                            C_mask[:], AluOpType.add), deps=[tok_diag, T_INIT], signal=True)
                        toks = [tok_sc, tok_mask]
                    tok_exp = P.add("scalar", lambda e, qt=qt, c0=c0, c1=c1, bi=bi, par=par, s3=s3: e.activation(
                        ATT_E[s3][:, c0:c1], PS[par][:, 0:c1 - c0],
                        AF.Exp, bias=MNEG[:, qt:qt + 1], scale=0.125,
                        accum_out=SROW[:, s3 * 2 + bi:s3 * 2 + bi + 1]),
                        deps=toks + [slot_cons[s3]], signal=True)
                    PM.read(ka, tok_exp); PM.read(kb, tok_exp)
                    toks_exp.append(tok_exp)
                return toks_exp

            def emit_tail_vec(qt, toks_exp):
                par = qt % 2
                s3 = qt % 3
                kw = (qt + 1) * 128
                n_blk = (kw + 1023) // 1024
                tok_s = toks_exp[-1]
                if n_blk > 1:
                    tok_s = P.add("vector", lambda e, s3=s3: e.tensor_tensor(
                        SROW[:, s3 * 2:s3 * 2 + 1], SROW[:, s3 * 2:s3 * 2 + 1],
                        SROW[:, s3 * 2 + 1:s3 * 2 + 2],
                        AluOpType.add), deps=toks_exp, signal=True)
                tok_ri = P.add("vector", lambda e, s3=s3: e.reciprocal(
                    SROW[:, 12 + s3:13 + s3], SROW[:, s3 * 2:s3 * 2 + 1]),
                    deps=[tok_s], signal=True)
                tok_rb = P.add("vector", lambda e, s3=s3: e.tensor_scalar(
                    RINVB[:, s3:s3 + 1], SROW[:, 12 + s3:13 + s3], 1.0, None,
                    AluOpType.mult), deps=[slot_cons[s3], tok_ri], signal=True)
                tok_D = P.add("vector", lambda e, s3=s3: e.tensor_scalar(
                    DBUF[s3][:, :], C_identb[:], SROW[:, 12 + s3:13 + s3], None,
                    AluOpType.mult), deps=[T_INIT, slot_cons[s3], tok_ri], signal=True)
                return tok_rb, tok_D

            def emit_tail(qt, toks_exp, tok_rb, tok_D):
                par = qt % 2
                s3 = qt % 3
                kw = (qt + 1) * 128
                groups = list(enumerate(range(0, qt + 1, 4)))
                avc = 512 + (qt % 2) * 128
                tok_av = None

                def emit_av(gi, kt0):
                    nonlocal tok_av
                    kts = list(range(kt0, min(kt0 + 4, qt + 1)))
                    reg = gi % 2
                    dav = (PM.write_deps("p3b") if gi == 0 else []) + [cast_toks[gi]]
                    for kt in kts:
                        tok_av = P.add("tensor", lambda e, kt=kt, qt=qt, h=h, avc=avc, reg=reg: e.matmul(
                            PS[3][(h % 2) * 64:(h % 2) * 64 + 64, avc:avc + 128],
                            VT[:, kt * 1024 + h * 64: kt * 1024 + h * 64 + 64],
                            ATT_T[:, reg * 512 + (kt % 4) * 128: reg * 512 + (kt % 4 + 1) * 128],
                            start=(kt == 0), stop=(kt == qt),
                            tile_position=(0, (h % 2) * 64)),
                            deps=dav if kt == kts[0] else [],
                            signal=(kt == kts[-1]))
                    attt_cons[reg] = tok_av

                cast_toks = {}
                for gi, kt0 in groups:
                    kts = list(range(kt0, min(kt0 + 4, qt + 1)))
                    reg = gi % 2
                    pk = f"p2{'ab'[reg]}"
                    dT = PM.write_deps(pk) + [tok_D] + toks_exp
                    tok_T = None
                    tok_mm2 = None
                    for kt in kts:
                        tok_T = P.add("tensor", lambda e, kt=kt, s3=s3, reg=reg: e.matmul(
                            PS[2][:, reg * 512 + (kt % 4) * 128: reg * 512 + (kt % 4 + 1) * 128],
                            ATT_E[s3][:, kt * 128:(kt + 1) * 128],
                            DBUF[s3][:, :], start=True, stop=True),
                            deps=dT if kt == kt0 else [], signal=(kt == kts[-1]))
                        tok_mm2 = P.add("tensor", lambda e, kt=kt, s3=s3: e.matmul(
                            PS[3][:, kt:kt + 1],
                            ATT_E[s3][:, kt * 128:(kt + 1) * 128],
                            RINVB[:, s3:s3 + 1],
                            start=(qt == 0 and kt == 0),
                            stop=(qt == 15 and kt == 15),
                            skip_group_check=True),
                            deps=(PM.write_deps("p3a") + [tok_rb]) if (qt == 0 and kt == 0) else [],
                            signal=(qt == 15 and kt == 15))
                    PM.wrote(pk, tok_T)
                    if qt == 15 and kts[-1] == 15:
                        PM.wrote("p3a", tok_mm2)
                    dTe = PM.read_deps(pk) + [attt_cons[reg]]
                    if gi % 2 == 0:
                        tok_Te = P.add("vector", lambda e, n=len(kts), reg=reg: e.tensor_copy(
                            ATT_T[:, reg * 512: reg * 512 + n * 128],
                            PS[2][:, reg * 512: reg * 512 + n * 128]), deps=dTe, signal=True)
                    else:
                        tok_Te = P.add("vector", lambda e, n=len(kts), reg=reg: e.tensor_copy(
                            ATT_T[:, reg * 512: reg * 512 + n * 128],
                            PS[2][:, reg * 512: reg * 512 + n * 128]), deps=dTe, signal=True)
                    PM.read(pk, tok_Te)
                    cast_toks[gi] = tok_Te
                    if gi >= 1:
                        emit_av(gi - 1, groups[gi - 1][1])
                emit_av(groups[-1][0], groups[-1][1])
                PM.wrote("p3b", tok_av)
                slot_cons[s3] = tok_av
                cons.append(tok_av)
                if qt % 2 == 1:
                    dce = PM.read_deps("p3b")
                    tok_ce = P.add("vector", lambda e, qt=qt, h=h: e.tensor_copy(
                        CAT[(h % 2) * 64:(h % 2) * 64 + 64,
                            (h // 2) * T + (qt - 1) * 128:(h // 2) * T + (qt + 1) * 128],
                        PS[3][(h % 2) * 64:(h % 2) * 64 + 64, 512:768]),
                        deps=dce, signal=True)
                    PM.read("p3b", tok_ce)
                    cons.append(tok_ce)

            pend = []
            for qt in range(16):
                if pend and pend[0][0] == qt - 2:
                    item = pend[0]
                    pend[0] = item + emit_tail_vec(item[0], item[1])
                toks = emit_scores(qt)
                if len(pend) >= 2:
                    emit_tail(*pend.pop(0))
                pend.append((qt, toks))
            if inject is not None:
                inject()
            for item in pend:
                item = item + emit_tail_vec(item[0], item[1])
                emit_tail(*item)
            ext_cons[b] = cons
            return cons

        def head_cs(h):
            d3 = PM.read_deps("p3a")
            tok_cs = P.add("vector", lambda e: e.tensor_copy(
                CS_SB[:, :], PS[3][:, 0:16]), deps=d3 + [cs_cons[0]], signal=True)
            PM.read("p3a", tok_cs)
            tok_b = None
            tok_mt_all = None
            for rnd in range(2):
                dmt = PM.write_deps("p0a") + PM.write_deps("p0b") + [tok_cs, T_INIT]
                tok_mt = None
                for kti in range(8):
                    kt = rnd * 8 + kti
                    tok_mt = P.add("tensor", lambda e, kt=kt, kti=kti: e.matmul(
                        PS[0][0:1, kti * 128:(kti + 1) * 128],
                        CS_SB[:, kt:kt + 1], C_identb[:], start=True, stop=True),
                        deps=dmt if kti == 0 else [], signal=(kti == 7))
                PM.wrote("p0a", tok_mt); PM.wrote("p0b", tok_mt)
                tok_mt_all = tok_mt
                dbe = PM.read_deps("p0a")
                tok_b = P.add("scalar", lambda e, rnd=rnd, h=h: e.activation(
                    KEXT[(h + 1) % 2][64:65, rnd * 1024:(rnd + 1) * 1024],
                    PS[0][0:1, 0:1024], AF.Copy,
                    scale=WVROW[0:1, h:h + 1]), deps=dbe + ext_cons[(h + 1) % 2],
                    signal=True)
                PM.read("p0a", tok_b); PM.read("p0b", tok_b)
            cs_cons[0] = tok_mt_all
            tok_bias[h + 1] = tok_b

        P.force_drain = False
        prepare_head(0)
        for h in range(NH):
            head_qt_loop(h, inject=(
                (lambda hh=h: prepare_head(hh + 1)) if h + 1 < NH else None))
            if h + 1 < NH:
                head_cs(h)
        P.force_drain = True
        lsc_hd.close()
        lsc_qkv.close()

        lsc_pr = contextlib.ExitStack()
        X2 = lsc_pr.enter_context(nc.sbuf_tensor("X2", [128, NC * T], F32))
        lsc_wp = contextlib.ExitStack()
        WPB = lsc_wp.enter_context(nc.sbuf_tensor("wpb", [128, NC * 1024], BF16))
        tok_rst = P.add("sync", lambda e: e.dma_start(
            out=X2.rearrange("p (c t) -> p c t", c=NC),
            in_=xspill.rearrange("(c p) t -> p c t", p=128)),
            deps=[tok_spill] + snapshot(), dma="spill")
        tok_wp = P.add("gpsimd", lambda e: e.dma_start(
            out=WPB.rearrange("p (c n) -> p c n", c=NC),
            in_=dr["wproj"].rearrange("(c p) n -> p c n", p=128)),
            deps=snapshot(), dma="wqkv")
        tok_pr_last = None
        for db in range(NC):
            pkeys = [["p0a", "p0b"], ["p1a", "p1b"]][db % 2]
            pidx = db % 2
            for tc in range(4):
                dpm = []
                for pk in pkeys:
                    dpm += PM.write_deps(pk)
                dpm += [tok_wp]
                tok_pm = None
                for c in range(NC):
                    tok_pm = P.add("tensor", lambda e, db=db, tc=tc, c=c, pidx=pidx: e.matmul(
                        PS[pidx][:, (tc % 2) * 512:(tc % 2 + 1) * 512],
                        WPB[:, c * 1024 + db * 128: c * 1024 + (db + 1) * 128],
                        CAT[:, c * T + tc * 512: c * T + (tc + 1) * 512],
                        start=(c == 0), stop=(c == NC - 1)),
                        deps=dpm if c == 0 else [], signal=(c == NC - 1))
                for pk in pkeys:
                    PM.wrote(pk, tok_pm)
                dpe = []
                for pk in pkeys:
                    dpe += PM.read_deps(pk)
                tok_pe = P.add("vector", lambda e, db=db, tc=tc, pidx=pidx: e.tensor_tensor(
                    X2[:, db * T + tc * 512: db * T + (tc + 1) * 512],
                    X2[:, db * T + tc * 512: db * T + (tc + 1) * 512],
                    PS[pidx][:, (tc % 2) * 512:(tc % 2 + 1) * 512], AluOpType.add),
                    deps=dpe + [tok_rst], signal=True)
                for pk in pkeys:
                    PM.read(pk, tok_pe)
                if tc == 1:
                    st["Xc"][(0, db)] = tok_pe
                elif tc == 3:
                    st["Xc"][(1, db)] = tok_pe
                tok_pr_last = tok_pe
        st["Xh"] = {0: tok_pr_last, 1: tok_pr_last}
        lsc_wp.close()
        rsc_cat.close()
        for k in ("p0", "p1", "p2", "p3"):
            PM.read(k, tok_pr_last)
        st["X"] = tok_pr_last
        return X2, lsc_pr

    def snapshot():
        toks = [("x", s, P.ct[s], 0) for s in ("sT", "sV", "sS", "sG") if P.ct[s] > 0]
        toks += [("x", s, P.ct[s], 0) for s in sorted(P.dma_sems) if P.ct.get(s, 0) > 0]
        return toks

    with nc.Block() as block:
        def open_ffn_scope(tag):
            sc = contextlib.ExitStack()
            f = lambda n, s, dt: sc.enter_context(nc.sbuf_tensor(n + tag, s, dt))
            t = dict(
                XNt=f("XN", [128, NC * 1024], BF16),
                warena=f("warena", [128, 8192], BF16),
                s_sb=f("s_sb", [128, 16 * 1024], BF16),
                acts=f("acts", [128, 16 * 1024], BF16),
                xsq=f("xsq", [128, 2048], F32R),
                scr=f("scr", [128, 1024], F32),
                silu_sb=f("silu_sb", [128, 4096], BF16),
                sgb=f("sgb", [128, 2048], BF16),
                cwb=f("cwb", [128, 1024], BF16),
            )
            return sc, t

        if "ffnA" in stages:
            sc1, tt = open_ffn_scope("1")
            ffn_call(X, **tt)
            sc1.close()
        XOUT = X
        _endscopes = []
        if "attn" in stages:
            XOUT, _prscope = attention(snapshot())
            _endscopes.append(_prscope)
        if "ffnC" in stages:
            sc2, tt = open_ffn_scope("2")
            ffn_call(XOUT, guard=snapshot(), **tt)
            sc2.close()
        tok_out = P.add("sync", lambda e, XO=XOUT: e.dma_start(
            out=yout.rearrange("(c p) t -> p c t", p=128),
            in_=XO.rearrange("p (c t) -> p c t", c=NC)),
            deps=[st["X"]] + snapshot(), dma="out")
        P.add("sync", lambda e: None, deps=[tok_out])
        for s in sorted(P.dma_sems):
            sems[s] = est.enter_context(nc.semaphore(s))
        P.emit(block, sems)
        for _s in reversed(_endscopes):
            _s.close()
        xsc1.close()
    est.close()
    return nc


def kernel(**inputs):
    in_maps = _host_prep(inputs)
    nc = build_program()
    res = run_bass_kernel_spmd(nc, in_maps, core_ids=list(range(8)))
    return np.stack([np.ascontiguousarray(r["yout"].T) for r in res.results], 0)


# version 19
# speedup vs baseline: 1.0529x; 1.0529x over previous
import math
import contextlib
import numpy as np

import concourse.bass as bass
import concourse.mybir as mybir
from concourse.bass_utils import run_bass_kernel_spmd
from concourse.alu_op_type import AluOpType

F32 = mybir.dt.float32
F32R = mybir.dt.float32r
BF16 = mybir.dt.bfloat16
AF = mybir.ActivationFunctionType
AX = mybir.AxisListType

T = 2048
D = 1024
NH = 16
HD = 64
NC = D // 128
SI = 2048
NE = 8
EI = 512
HALF = T // 2
ROPE_BASE = 10000.0
QK_SCALE = 1.0 / math.sqrt(8.0)
LN_EPS = 1e-5

ENGINES = ("tensor", "vector", "scalar", "sync", "gpsimd")


def _host_constants():
    import ml_dtypes
    c = {}
    invf32 = np.array([
        0x3f800000, 0x3f3ff911, 0x3f0ff59a, 0x3ed7e89b, 0x3ea1e89b, 0x3e72d423,
        0x3e361887, 0x3e088d77, 0x3dcccccd, 0x3d99940d, 0x3d6655c2, 0x3d2cba15,
        0x3d0186e3, 0x3cc2434f, 0x3c91ad39, 0x3c5a7bf2, 0x3c23d70a, 0x3bf5b9b0,
        0x3bb8449c, 0x3b8a2e77, 0x3b4f3e38, 0x3b1b690d, 0x3ae91528, 0x3aaec98e,
        0x3a83126f, 0x3a44948c, 0x3a136a16, 0x39dd1725, 0x39a5cb60, 0x3978a815,
        0x393a7753, 0x390bd472], dtype=np.uint32).view(np.float32)
    t32 = np.arange(T, dtype=np.float32)
    cos_l, sin_l = [], []
    for i in range(NH):
        fsc = np.float32(ROPE_BASE ** (i / NH))
        f32 = (invf32 * fsc).astype(np.float32)
        ang = (t32[:, None] * f32[None, :]).astype(np.float32)
        cos_l.append(np.cos(ang, dtype=np.float32))
        sin_l.append(np.sin(ang, dtype=np.float32))
    c["cos_t"] = np.stack(cos_l, 1).reshape(T, NH * 32).astype(np.float32)
    c["sin_t"] = np.stack(sin_l, 1).reshape(T, NH * 32).astype(np.float32)
    c["ident_bf"] = np.eye(128, dtype=ml_dtypes.bfloat16)
    c["ident_f"] = np.eye(128, dtype=np.float32)
    c["ones_f"] = np.ones((128, 1), np.float32)
    c["ones_r"] = np.ones((128, 1), np.float32)
    c["ones_row_f"] = np.ones((1, 128), np.float32)
    c["ones_row_bf"] = np.ones((1, 128), ml_dtypes.bfloat16)
    c["ones64_bf"] = np.ones((64, 1), ml_dtypes.bfloat16)
    c["ones128_bf"] = np.ones((128, 1), ml_dtypes.bfloat16)
    m = np.zeros((128, 128), np.float32)
    m[np.triu_indices(128, k=1)] = -1.0e30
    c["diag_mask"] = m
    oh = np.zeros((9, 9 * 128), np.float32)
    for e in range(9):
        oh[e, e * 128:(e + 1) * 128] = 1.0
    c["onehot9"] = oh.astype(ml_dtypes.bfloat16)
    return c


def _host_prep(inputs):
    A = np.ascontiguousarray
    x = inputs["L0_attn_x"]
    wqkv = inputs["L0_attn_Wqkv"]
    idx = np.arange(3 * D)
    for base in (0, D):
        blk = []
        for par in (0, 1):
            for h in range(NH):
                blk.extend(range(base + h * HD + par, base + (h + 1) * HD, 2))
        idx[base:base + D] = blk
    w = {
        "wqkv": A(wqkv[:, idx]),
        "wproj": A(inputs["L0_attn_Wproj"]),
        "casc": A(inputs["L0_attn_cascade_w"].reshape(1, NH - 1)),
        "wgT": A(inputs["L0_ffn_shared_gate_w"].T),
        "wuT": A(inputs["L0_ffn_shared_up_w"].T),
        "wdT": A(inputs["L0_ffn_shared_down_w"].T),
        "wguT": A(inputs["L0_ffn_gate_up_proj"].transpose(0, 2, 1)),
        "wdnT": A(inputs["L0_ffn_down_proj"].transpose(0, 2, 1)),
        "wr9": A(np.concatenate([inputs["L0_ffn_router_weight"],
                                 inputs["L0_ffn_shared_expert_gate_w"]], 0).T),
    }
    w.update(_host_constants())
    return [dict(w, xin=A(x[b].T)) for b in range(8)]


class Prog:

    def __init__(self):
        self.ev = {e: [] for e in ENGINES}
        self.ct = {"sT": 0, "sV": 0, "sS": 0, "sG": 0}
        self.eng_sem = {"tensor": "sT", "vector": "sV", "scalar": "sS",
                        "gpsimd": "sG"}
        self.dma_sems = set()
        self.nev = {e: 0 for e in ENGINES}
        self.drained_upto = {e: -1 for e in ENGINES}
        self.force_drain = True

    def add(self, eng, fn, deps=(), signal=False, dma=False, drain=False):
        waits = {}
        drainable = eng in ("vector", "scalar")
        need_drain = (bool(drain) or self.force_drain) and drainable
        for d in deps:
            if d is None:
                continue
            deng, dsem, dval, didx = d
            if deng == eng:
                if drainable and didx > self.drained_upto[eng]:
                    need_drain = True
                continue
            assert dval is not None, f"cross-engine dep unsignalled ({deng}->{eng})"
            waits[dsem] = max(waits.get(dsem, 0), dval)
        incs = []
        val = None
        if dma:
            sem = "d_" + dma
            self.dma_sems.add(sem)
            self.ct[sem] = self.ct.get(sem, 0) + 16
            val = self.ct[sem]
            incs.append((sem, 16))
        elif signal:
            sem = self.eng_sem[eng]
            self.ct[sem] += 1
            val = self.ct[sem]
            incs.append((sem, 1))
        else:
            sem = self.eng_sem.get(eng, "sT")
        idx = self.nev[eng]
        self.nev[eng] += 1
        if need_drain:
            self.drained_upto[eng] = idx - 1
        self.ev[eng].append((fn, waits, incs, need_drain))
        return (eng, sem, val, idx)

    def emit(self, block, sems):
        for eng_name in ENGINES:
            events = self.ev[eng_name]

            def run(eng, events=events, eng_name=eng_name):
                last_wait = {}
                for i, (fn, waits, incs, need_drain) in enumerate(events):
                    for sem_name in sorted(waits):
                        v = waits[sem_name]
                        if last_wait.get(sem_name, -1) < v:
                            eng.wait_ge(sems[sem_name], v)
                            last_wait[sem_name] = v
                    if need_drain:
                        eng.drain()
                    inst = fn(eng)
                    for sem_name, k in incs:
                        assert inst is not None, f"signal on None inst ({eng_name} #{i})"
                        inst.then_inc(sems[sem_name], k)

            getattr(block, eng_name)(run)


class PsumMgr:
    def __init__(self):
        self.state = {}

    def _st(self, key):
        return self.state.setdefault(key, {"w": None, "r": []})

    def write_deps(self, key):
        return list(self._st(key)["r"])

    def wrote(self, key, tok):
        s = self._st(key)
        s["w"] = tok
        s["r"] = []

    def read_deps(self, key):
        w = self._st(key)["w"]
        return [w] if w is not None else []

    def read(self, key, tok):
        self._st(key)["r"].append(tok)


def build_program(stages=("ffnA", "attn", "ffnC")):
    nc = bass.Bass("TRN2", target_bir_lowering=False, debug=False)
    P = Prog()
    PM = PsumMgr()

    dr = {}
    def din(name, shape, dt=F32):
        dr[name] = nc.dram_tensor(name, shape, dt, kind="ExternalInput").ap()
    din("xin", [D, T])
    din("wqkv", [D, 3 * D]); din("wproj", [D, D]); din("casc", [1, NH - 1])
    din("wgT", [D, SI]); din("wuT", [D, SI]); din("wdT", [SI, D])
    din("wguT", [NE, D, D]); din("wdnT", [NE, EI, D]); din("wr9", [D, 9])
    din("cos_t", [T, 512]); din("sin_t", [T, 512])
    din("ident_bf", [128, 128], BF16); din("ident_f", [128, 128])
    din("ones_f", [128, 1]); din("ones_r", [128, 1], F32R)
    din("ones_row_f", [1, 128]); din("ones_row_bf", [1, 128], BF16)
    din("ones64_bf", [64, 1], BF16); din("diag_mask", [128, 128])
    din("ones128_bf", [128, 1], BF16)
    din("onehot9", [9, 9 * 128], BF16)
    yout = nc.dram_tensor("yout", [D, T], F32, kind="ExternalOutput").ap()
    xspill = nc.dram_tensor("xspill", [D, T], F32, kind="Internal").ap()

    est = contextlib.ExitStack()
    sb = lambda n, s, dt: est.enter_context(nc.sbuf_tensor(n, s, dt))
    sems = {s: est.enter_context(nc.semaphore(s)) for s in ("sT", "sV", "sS", "sG")}

    STATS = sb("STATS", [1, 2048], F32)
    C_onesf = sb("c_onesf", [128, 1], F32)
    C_onesr = sb("c_onesr", [128, 1], F32R)
    C_orowf = sb("c_orowf", [1, 128], F32)
    C_orowbf = sb("c_orowbf", [1, 128], BF16)
    C_ones64 = sb("c_ones64", [64, 1], BF16)
    C_ones128b = sb("c_ones128b", [128, 1], BF16)
    C_identb = sb("c_identb", [128, 128], BF16)
    C_identf = sb("c_identf", [128, 128], F32)
    C_mask = sb("c_mask", [128, 128], F32)
    C_casc = sb("c_casc", [1, NH - 1], F32)
    WR9 = sb("wr9_s", [128, NC * 9], BF16)
    CWFM2 = [sb("cwfm0", [9, 1024], BF16), sb("cwfm1", [9, 1024], BF16)]
    C_oh9 = sb("c_oh9", [9, 9 * 128], BF16)
    WVROW = sb("wvrow", [1, 32], F32)
    PS = [est.enter_context(nc.psum_tensor(f"pb{i}", [128, 1024], F32)) for i in range(4)]
    xsc1 = contextlib.ExitStack()
    X = xsc1.enter_context(nc.sbuf_tensor("X1", [128, NC * T], F32))

    for nm, dst in (("ones_f", C_onesf), ("ones_row_f", C_orowf), ("diag_mask", C_mask),
                    ("casc", C_casc), ("ident_f", C_identf), ("ones_r", C_onesr),
                    ("ident_bf", C_identb), ("ones_row_bf", C_orowbf),
                    ("ones64_bf", C_ones64), ("onehot9", C_oh9),
                    ("ones128_bf", C_ones128b)):
        P.add("sync", lambda e, d=dst, s=dr[nm]: e.dma_start(out=d[:], in_=s), dma="init")
    T_INIT = ("sync", "d_init", P.ct["d_init"], 0)
    t_xc = {}
    for c in range(NC):
        t_xc[c] = P.add("sync", lambda e, c=c: e.dma_start(
            out=X[:, c * T:(c + 1) * T],
            in_=dr["xin"][c * 128:(c + 1) * 128, :]), dma="init")
    t_wr9 = P.add("gpsimd", lambda e: e.dma_start(
        out=WR9.rearrange("p (c n) -> p c n", c=NC),
        in_=dr["wr9"].rearrange("(c p) n -> p c n", p=128)), dma="winit")
    t_onesf = t_orowf = t_mask = t_casc = t_identf = T_INIT
    t_onesr = t_identb = t_orowbf = t_ones64 = t_oh9 = T_INIT
    t_X = t_xc[NC - 1]

    t_wz = P.add("vector", lambda e: e.memset(WVROW[:, :], 0.0), signal=True)
    P.add("vector", lambda e: e.tensor_scalar(
        WVROW[:, 0:NH - 1], C_casc[:], 8.0 / T, None, AluOpType.mult), deps=[T_INIT])
    t_wv = P.add("scalar", lambda e: e.activation(
        WVROW[:, 16:16 + NH - 1], C_casc[:], AF.Abs), deps=[T_INIT, t_wz], signal=True)

    st = {"X": t_X, "Xh": {0: t_X, 1: t_X},
          "Xc": {(hh, c): t_xc[c] for hh in range(2) for c in range(NC)}}

    def layer_norm(XT, XNt, h0, xsq, scr, xn_w=1024, xn_off=0, extra=()):
        ht = 1024
        tok_sq_prev = [None, None]
        tok_mm1 = tok_mm2 = None
        hh = 0 if h0 < HALF else 1
        xdep_c = lambda c: st["Xc"].get((hh, c), st["X"])
        d0 = (PM.write_deps("p0") + PM.write_deps("p1")
              + [t_onesf, t_onesr] + list(extra))
        for c in range(NC):
            slot = c % 2
            xsl = XT[:, c * T + h0: c * T + h0 + ht]
            xbs = XNt[:, c * xn_w + xn_off: c * xn_w + xn_off + 1024]
            tok_sq = P.add("scalar", lambda e, i=xsl, sl=slot: e.activation(
                xsq[:, sl * ht:(sl + 1) * ht], i, AF.Square),
                deps=[xdep_c(c), tok_sq_prev[slot]] + list(extra), signal=True)
            tok_xb = P.add("scalar", lambda e, i=xsl, o=xbs: e.copy(o, i),
                           deps=[xdep_c(c)], signal=True)
            for g in range(2):
                tok_mm1 = P.add("tensor", lambda e, c=c, g=g, i=xbs: e.matmul(
                    PS[0][0:1, g * 512:(g + 1) * 512], C_ones128b[:],
                    i[:, g * 512:(g + 1) * 512], start=(c == 0), stop=(c == NC - 1)),
                    deps=[tok_xb] + (d0 if c == 0 else []), signal=(c == NC - 1))
                tok_mm2 = P.add("tensor", lambda e, c=c, g=g, sl=slot: e.matmul(
                    PS[1][0:1, g * 512:(g + 1) * 512], C_onesr[:],
                    xsq[:, sl * ht + g * 512: sl * ht + (g + 1) * 512],
                    start=(c == 0), stop=(c == NC - 1)),
                    deps=[tok_sq] + (d0 if c == 0 else []), signal=(g == 1))
            tok_sq_prev[slot] = tok_mm2
        PM.wrote("p0", tok_mm1); PM.wrote("p1", tok_mm2)
        A_row, B_row = STATS[0:1, 0:1024], STATS[0:1, 1024:2048]
        m_row = scr[0:1, 0:1024]
        dv = PM.read_deps("p0") + PM.read_deps("p1")
        P.add("vector", lambda e: e.tensor_scalar(
            m_row[:, :], PS[0][0:1, 0:1024], 1.0 / D, None, AluOpType.mult), deps=dv)
        tok_e2 = P.add("vector", lambda e: e.tensor_scalar(
            B_row[:, :], PS[1][0:1, 0:1024], 1.0 / D, LN_EPS,
            AluOpType.mult, AluOpType.add), signal=True)
        PM.read("p0", tok_e2); PM.read("p1", tok_e2)
        P.add("vector", lambda e: e.scalar_tensor_tensor(
            A_row[:, :], m_row[:, :], -1.0, m_row[:, :],
            AluOpType.mult, AluOpType.mult))
        tok_var = P.add("vector", lambda e: e.tensor_tensor(
            B_row[:, :], B_row[:, :], A_row[:, :], AluOpType.add), signal=True)
        tok_sd = P.add("scalar", lambda e: e.activation(
            B_row[:, :], B_row[:, :], AF.Sqrt), deps=[tok_var], signal=True)
        tok_A = P.add("vector", lambda e: e.reciprocal(A_row[:, :], B_row[:, :]),
                      deps=[tok_sd], signal=True)
        tok_B = P.add("vector", lambda e: e.scalar_tensor_tensor(
            B_row[:, :], m_row[:, :], -1.0, A_row[:, :],
            AluOpType.mult, AluOpType.mult), signal=True)
        dA = PM.write_deps("p2") + [tok_A, t_orowf]
        dB = PM.write_deps("p3") + [tok_B]
        tok_rA = tok_rB = None
        for g in range(2):
            tok_rA = P.add("tensor", lambda e, g=g: e.matmul(
                PS[2][:, g * 512:(g + 1) * 512], C_orowf[:],
                A_row[:, g * 512:(g + 1) * 512], start=True, stop=True),
                deps=dA if g == 0 else [], signal=(g == 1))
            tok_rB = P.add("tensor", lambda e, g=g: e.matmul(
                PS[3][:, g * 512:(g + 1) * 512], C_orowf[:],
                B_row[:, g * 512:(g + 1) * 512], start=True, stop=True),
                deps=dB if g == 0 else [], signal=(g == 1))
        PM.wrote("p2", tok_rA); PM.wrote("p3", tok_rB)
        dAp = PM.read_deps("p2") + PM.read_deps("p3")
        tok_xn = None
        for c in range(NC):
            xsl = XT[:, c * T + h0: c * T + h0 + 1024]
            nsl = XNt[:, c * xn_w + xn_off: c * xn_w + xn_off + 1024]
            P.add("vector", lambda e, i=xsl: e.tensor_tensor(
                scr[:, 0:1024], i, PS[2][:, 0:1024], AluOpType.mult),
                deps=dAp if c == 0 else [])
            tok_xn = P.add("vector", lambda e, o=nsl: e.tensor_tensor(
                o, scr[:, 0:1024], PS[3][:, 0:1024], AluOpType.add),
                signal=(c == NC - 1))
        PM.read("p2", tok_xn); PM.read("p3", tok_xn)
        return tok_xn

    def router_half(XNt, h0, scr, tok_xn, CWFM):
        BIG = 1.0e30
        dW = PM.write_deps("p0") + [tok_xn, t_wr9]
        tok_lg = None
        for tt in range(8):
            for c in range(NC):
                tok_lg = P.add("tensor", lambda e, tt=tt, c=c: e.matmul(
                    PS[0][:, tt * 16: tt * 16 + 9],
                    XNt[:, c * 1024 + tt * 128: c * 1024 + (tt + 1) * 128],
                    WR9[:, c * 9:(c + 1) * 9], start=(c == 0), stop=(c == NC - 1)),
                    deps=dW if (tt == 0 and c == 0) else [], signal=(c == NC - 1))
        PM.wrote("p0", tok_lg)
        dl = PM.read_deps("p0")
        BIG = 1.0e30
        lg3 = PS[0].rearrange("p (t k) -> p t k", k=16)[:, 0:8, 0:8]
        sg3 = PS[0].rearrange("p (t k) -> p t k", k=16)[:, 0:8, 8:9]
        r3_ = lambda a, k: a.rearrange("p (t k) -> p t k", k=k)
        m1 = scr[:, 0:8]; m2 = scr[:, 8:16]; dd = scr[:, 16:24]
        sd = scr[:, 24:32]; sdb = scr[:, 32:40]; sgc = scr[:, 40:48]
        eq1 = scr[:, 64:128]; l2 = scr[:, 128:192]; eq2 = scr[:, 192:256]
        t1 = scr[:, 256:320]
        csg = scr[:, 320:392]
        P.add("vector", lambda e: e.tensor_reduce(
            r3_(m1, 1), lg3, AX.X, AluOpType.max), deps=dl)
        P.add("vector", lambda e: e.tensor_tensor(
            r3_(eq1, 8), lg3, r3_(m1, 1).broadcast_to((128, 8, 8)), AluOpType.is_equal))
        P.add("vector", lambda e: e.scalar_tensor_tensor(
            r3_(l2, 8), r3_(eq1, 8), -BIG, lg3, AluOpType.mult, AluOpType.add))
        P.add("vector", lambda e: e.tensor_reduce(
            r3_(m2, 1), r3_(l2, 8), AX.X, AluOpType.max))
        P.add("vector", lambda e: e.tensor_tensor(
            r3_(eq2, 8), r3_(l2, 8), r3_(m2, 1).broadcast_to((128, 8, 8)), AluOpType.is_equal))
        tok_d = P.add("vector", lambda e: e.tensor_tensor(
            dd, m1, m2, AluOpType.subtract), signal=True)
        P.add("scalar", lambda e: e.activation(sd, dd, AF.Sigmoid), deps=[tok_d])
        tok_sg = P.add("scalar", lambda e: e.activation(
            sgc, sg3.rearrange("p t k -> p (t k)") if False else sgc, AF.Sigmoid)
            if False else e.activation(sgc, PS[0].rearrange("p (t k) -> p t k", k=16)[:, 0:8, 8:9].rearrange("p t k -> p (t k)"), AF.Sigmoid), signal=True)
        P.add("vector", lambda e: e.tensor_scalar(
            sd, sd, 0.5, None, AluOpType.mult), deps=[tok_sg])
        P.add("vector", lambda e: e.tensor_scalar(
            sdb, sd, -1.0, 0.5, AluOpType.mult, AluOpType.add))
        P.add("vector", lambda e: e.tensor_scalar(
            r3_(csg, 9)[:, :, 8:9].rearrange("p t k -> p (t k)"), sgc, 0.5, None, AluOpType.mult))
        P.add("vector", lambda e: e.tensor_tensor(
            r3_(csg, 9)[:, :, 0:8], r3_(eq1, 8),
            r3_(sd, 1).broadcast_to((128, 8, 8)), AluOpType.mult))
        P.add("vector", lambda e: e.tensor_tensor(
            r3_(t1, 8), r3_(eq2, 8), r3_(sdb, 1).broadcast_to((128, 8, 8)), AluOpType.mult))
        tok_cw = P.add("vector", lambda e: e.tensor_tensor(
            r3_(csg, 9)[:, :, 0:8], r3_(csg, 9)[:, :, 0:8], r3_(t1, 8),
            AluOpType.add), signal=True)
        PM.read("p0", tok_cw); PM.read("p0", tok_sg)
        tok_cw_tiles = [tok_cw] * 8
        dtr = PM.write_deps("p1") + [t_identf]
        tok_tr = None
        for tt in range(8):
            csg_t = scr[:, 320 + 9 * tt: 320 + 9 * tt + 9]
            tok_tr = P.add("tensor", lambda e, tt=tt, csg=csg_t: e.matmul(
                PS[1][0:9, tt * 128:(tt + 1) * 128], csg, C_identf[:],
                start=True, stop=True),
                deps=[tok_cw_tiles[tt], tok_sg] + (dtr if tt == 0 else []), signal=(tt == 7))
        PM.wrote("p1", tok_tr)
        dtv = PM.read_deps("p1")
        tok_cwfm = P.add("vector", lambda e: e.tensor_copy(CWFM[:, :], PS[1][0:9, 0:1024]),
                         deps=dtv, signal=True)
        PM.read("p1", tok_cwfm)
        return tok_cwfm

    def ffn_call(XT, XNt, warena, s_sb, acts, xsq, scr, silu_sb, sgb, cwb, guard=()):
        R = [warena[:, i * 2048:(i + 1) * 2048] for i in range(4)]
        guard = list(guard)
        reg_free = [guard, guard, guard, guard]

        last_tok = {}

        def evac_gu(rep_sb, tok_rep, kind_idx, out_sl, sil_slot, sil_cons, pg, pu):
            ssl = silu_sb[:, sil_slot * 2048:(sil_slot + 1) * 2048]
            kg, ku = f"p{pg}", f"p{pu}"
            tok_sil = P.add("scalar", lambda e, ssl=ssl, pg=pg: e.activation(
                ssl[:, 0:1024], PS[pg][:, 0:1024], AF.Silu),
                deps=PM.read_deps(kg) + [sil_cons[sil_slot]], signal=True)
            PM.read(kg, tok_sil)
            tok_m1 = P.add("vector", lambda e, ssl=ssl, pu=pu: e.tensor_tensor(
                ssl[:, 1024:2048], ssl[:, 0:1024], PS[pu][:, 0:1024],
                AluOpType.mult), deps=PM.read_deps(ku) + [tok_sil], signal=True)
            PM.read(ku, tok_m1)
            tok_o = P.add("vector", lambda e, ssl=ssl, o=out_sl, r=rep_sb: e.tensor_tensor(
                o, ssl[:, 1024:2048], r[:, 0:1024], AluOpType.mult),
                deps=[tok_rep], signal=True)
            sil_cons[sil_slot] = tok_o
            last_tok[kind_idx] = tok_o
            return tok_o

        def ln_router(half, extra=()):
            h0 = half * HALF
            tok_xn = layer_norm(XT, XNt, h0, xsq, scr, extra=extra)
            tok_cwfm = router_half(XNt, h0, scr, tok_xn, CWFM2[half])
            dsg = PM.write_deps("p3") + [tok_cwfm, t_oh9]
            tok_sgr = None
            for g in range(2):
                tok_sgr = P.add("tensor", lambda e, g=g, half=half: e.matmul(
                    PS[3][:, g * 512:(g + 1) * 512], C_oh9[:, 8 * 128:9 * 128],
                    CWFM2[half][:, g * 512:(g + 1) * 512], start=True, stop=True),
                    deps=dsg if g == 0 else [], signal=(g == 1))
            PM.wrote("p3", tok_sgr)
            tok_sgb = P.add("vector", lambda e, half=half: e.tensor_copy(
                sgb[:, half * 1024:(half + 1) * 1024], PS[3][:, 0:1024]),
                deps=PM.read_deps("p3"), signal=True)
            PM.read("p3", tok_sgb)
            return tok_xn, tok_sgb, tok_cwfm

        pre = {0: ln_router(0, extra=guard)}
        for half in range(2):
            h0 = half * HALF
            tok_xn, tok_sgb, tok_cwfm = pre[half]
            sgb_h = sgb[:, half * 1024:(half + 1) * 1024]
            XN_OFF = 0
            sil_cons = [None, None]
            for blk in range(8):
                ra, rb = (0, 1) if blk % 2 == 0 else (2, 3)
                tok_wg = P.add("gpsimd", lambda e, blk=blk, d=R[ra]: e.dma_start(
                    out=d.rearrange("p (c n) -> p c n", c=NC),
                    in_=dr["wgT"].rearrange("(c p) s -> p c s", p=128)
                    [:, :, blk * 256:(blk + 1) * 256]),
                    deps=reg_free[ra], dma=f"w{ra}")
                tok_wu = P.add("gpsimd", lambda e, blk=blk, d=R[rb]: e.dma_start(
                    out=d.rearrange("p (c n) -> p c n", c=NC),
                    in_=dr["wuT"].rearrange("(c p) s -> p c s", p=128)
                    [:, :, blk * 256:(blk + 1) * 256]),
                    deps=reg_free[rb], dma=f"w{rb}")
                tok_g = tok_u = None
                for sci in range(2):
                    sc = blk * 2 + sci
                    pg, pu = (0, 1) if sc % 2 == 0 else (2, 3)
                    dg = PM.write_deps(f"p{pg}") + [tok_wg, tok_xn]
                    du = PM.write_deps(f"p{pu}") + [tok_wu]
                    for tc in range(2):
                        for c in range(NC):
                            xsl = XNt[:, c * 1024 + tc * 512: c * 1024 + (tc + 1) * 512]
                            tok_g = P.add("tensor", lambda e, c=c, tc=tc, sci=sci, d=R[ra], x=xsl, pg=pg: e.matmul(
                                PS[pg][:, tc * 512:(tc + 1) * 512],
                                d[:, c * 256 + sci * 128: c * 256 + (sci + 1) * 128], x,
                                start=(c == 0), stop=(c == NC - 1)),
                                deps=dg if (tc == 0 and c == 0) else [],
                                signal=(tc == 1 and c == NC - 1))
                            tok_u = P.add("tensor", lambda e, c=c, tc=tc, sci=sci, d=R[rb], x=xsl, pu=pu: e.matmul(
                                PS[pu][:, tc * 512:(tc + 1) * 512],
                                d[:, c * 256 + sci * 128: c * 256 + (sci + 1) * 128], x,
                                start=(c == 0), stop=(c == NC - 1)),
                                deps=du if (tc == 0 and c == 0) else [],
                                signal=(tc == 1 and c == NC - 1))
                    PM.wrote(f"p{pg}", tok_g); PM.wrote(f"p{pu}", tok_u)
                    evac_gu(sgb_h, tok_sgb, 2, s_sb[:, sc * 1024:(sc + 1) * 1024],
                            sc % 2, sil_cons, pg, pu)
                reg_free[ra] = [tok_g]
                reg_free[rb] = [tok_u]

            for grp in range(2):
                for ei in range(4):
                    e_id = grp * 4 + ei
                    dcw = PM.write_deps("p3") + [tok_cwfm, t_oh9]
                    tok_cwr = None
                    for g in range(2):
                        tok_cwr = P.add("tensor", lambda e, g=g, e_id=e_id, half=half: e.matmul(
                            PS[3][:, g * 512:(g + 1) * 512], C_oh9[:, e_id * 128:(e_id + 1) * 128],
                            CWFM2[half][:, g * 512:(g + 1) * 512],
                            start=True, stop=True),
                            deps=dcw if g == 0 else [], signal=(g == 1))
                    PM.wrote("p3", tok_cwr)
                    tok_cwb = P.add("vector", lambda e: e.tensor_copy(cwb[:, :], PS[3][:, 0:1024]),
                                    deps=PM.read_deps("p3"), signal=True)
                    PM.read("p3", tok_cwb)
                    for hf in range(2):
                        u = (ei * 2 + hf) % 2
                        ra, rb = (0, 1) if u == 0 else (2, 3)
                        tok_wg = P.add("gpsimd", lambda e, e_id=e_id, hf=hf, d=R[ra]: e.dma_start(
                            out=d.rearrange("p (c n) -> p c n", c=NC),
                            in_=dr["wguT"].rearrange("E (c p) n -> p (E c) n", p=128)
                            [:, e_id * NC:(e_id + 1) * NC, hf * 256:(hf + 1) * 256]),
                            deps=reg_free[ra], dma=f"w{ra}")
                        tok_wu = P.add("gpsimd", lambda e, e_id=e_id, hf=hf, d=R[rb]: e.dma_start(
                            out=d.rearrange("p (c n) -> p c n", c=NC),
                            in_=dr["wguT"].rearrange("E (c p) n -> p (E c) n", p=128)
                            [:, e_id * NC:(e_id + 1) * NC, 512 + hf * 256: 512 + (hf + 1) * 256]),
                            deps=reg_free[rb], dma=f"w{rb}")
                        tok_gt = tok_up = None
                        for pri in range(2):
                            pr = hf * 2 + pri
                            k = ei * 4 + pr
                            pg, pu = (0, 1) if k % 2 == 0 else (2, 3)
                            dgt = PM.write_deps(f"p{pg}") + [tok_wg, tok_xn]
                            dup = PM.write_deps(f"p{pu}") + [tok_wu]
                            for tc in range(2):
                                for c in range(NC):
                                    xsl = XNt[:, c * 1024 + tc * 512: c * 1024 + (tc + 1) * 512]
                                    tok_gt = P.add("tensor", lambda e, c=c, tc=tc, pri=pri, d=R[ra], x=xsl, pg=pg: e.matmul(
                                        PS[pg][:, tc * 512:(tc + 1) * 512],
                                        d[:, c * 256 + pri * 128: c * 256 + (pri + 1) * 128], x,
                                        start=(c == 0), stop=(c == NC - 1)),
                                        deps=dgt if (tc == 0 and c == 0) else [],
                                        signal=(tc == 1 and c == NC - 1))
                                    tok_up = P.add("tensor", lambda e, c=c, tc=tc, pri=pri, d=R[rb], x=xsl, pu=pu: e.matmul(
                                        PS[pu][:, tc * 512:(tc + 1) * 512],
                                        d[:, c * 256 + pri * 128: c * 256 + (pri + 1) * 128], x,
                                        start=(c == 0), stop=(c == NC - 1)),
                                        deps=dup if (tc == 0 and c == 0) else [],
                                        signal=(tc == 1 and c == NC - 1))
                            PM.wrote(f"p{pg}", tok_gt); PM.wrote(f"p{pu}", tok_up)
                            evac_gu(cwb, tok_cwb, 3, acts[:, k * 1024:(k + 1) * 1024],
                                    k % 2, sil_cons, pg, pu)
                        reg_free[ra] = [tok_gt]
                        reg_free[rb] = [tok_up]

                if grp == 1 and half == 0:
                    pre[1] = ln_router(1)
                for hb in range(NC):
                    ra, rb = (0, 1) if hb % 2 == 0 else (2, 3)
                    toks_w = []
                    if grp == 0:
                        toks_w.append(P.add("gpsimd", lambda e, hb=hb, d=R[ra]: e.dma_start(
                            out=d.rearrange("p (k n) -> p k n", k=16),
                            in_=dr["wdT"].rearrange("(k p) n -> p k n", p=128)
                            [:, :, hb * 128:(hb + 1) * 128]),
                            deps=reg_free[ra], dma=f"w{ra}"))
                    toks_w.append(P.add("gpsimd", lambda e, hb=hb, grp=grp, d=R[rb]: e.dma_start(
                        out=d.rearrange("p (k n) -> p k n", k=16),
                        in_=dr["wdnT"].rearrange("E (c p) n -> p (E c) n", p=128)
                        [:, grp * 16:(grp + 1) * 16, hb * 128:(hb + 1) * 128]),
                        deps=reg_free[rb], dma=f"w{rb}"))
                    pidx = hb % 4
                    pk = f"p{pidx}"
                    steps = ([("s", k) for k in range(16)] if grp == 0 else []) \
                        + [("a", k) for k in range(16)]
                    n_st = len(steps)
                    dmm = (PM.write_deps(pk) + toks_w + [tok_xn]
                           + [last_tok.get(2), last_tok.get(3)])
                    tok_d = None
                    for si, (kind, k) in enumerate(steps):
                        lhs_reg = R[ra] if kind == "s" else R[rb]
                        rhs_t = s_sb if kind == "s" else acts
                        for tc in range(2):
                            tok_d = P.add("tensor", lambda e, lhs_reg=lhs_reg, rhs_t=rhs_t,
                                          k=k, tc=tc, si=si, pidx=pidx, n_st=n_st: e.matmul(
                                PS[pidx][:, tc * 512:(tc + 1) * 512],
                                lhs_reg[:, k * 128:(k + 1) * 128],
                                rhs_t[:, k * 1024 + tc * 512: k * 1024 + (tc + 1) * 512],
                                start=(si == 0), stop=(si == n_st - 1)),
                                deps=dmm if si == 0 else [],
                                signal=(si == n_st - 1 and tc == 1))
                    PM.wrote(pk, tok_d)
                    if grp == 0:
                        reg_free[ra] = [tok_d]
                    reg_free[rb] = [tok_d]
                    dev = PM.read_deps(pk)
                    tok_ev = P.add("vector", lambda e, hb=hb, pidx=pidx, h0=h0, XT=XT: e.tensor_tensor(
                        XT[:, hb * T + h0: hb * T + h0 + 1024],
                        XT[:, hb * T + h0: hb * T + h0 + 1024],
                        PS[pidx][:, 0:1024], AluOpType.add), deps=dev, signal=True)
                    PM.read(pk, tok_ev)
                    st["X"] = tok_ev
                    st["Xh"][half] = tok_ev
                    st["Xc"][(half, hb)] = tok_ev


    def attention(guard):
        rsc_cat = contextlib.ExitStack()
        CAT = rsc_cat.enter_context(nc.sbuf_tensor("cat", [128, NC * T], BF16, side="right"))
        rsc_qkvm = contextlib.ExitStack()
        rbq = lambda n, s, dt: rsc_qkvm.enter_context(nc.sbuf_tensor(n, s, dt, side="right"))
        XNA = rbq("xna", [128, NC * T], BF16)
        rsc_ln = contextlib.ExitStack()
        xsq_a = rsc_ln.enter_context(nc.sbuf_tensor("xsq_a", [128, 2048], F32R, side="right"))
        scr_a = rsc_ln.enter_context(nc.sbuf_tensor("scr_a", [128, 1024], F32, side="right"))
        tok_ln = None
        for th in range(2):
            tok_ln = layer_norm(X, XNA, th * 1024, xsq_a, scr_a, xn_w=T,
                                xn_off=th * 1024, extra=guard)
        tok_spill = None
        for c in range(NC):
            tok_spill = P.add("sync", lambda e, c=c: e.dma_start(
                out=xspill[c * 128:(c + 1) * 128, :],
                in_=X[:, c * T:(c + 1) * T]),
                deps=[st["Xc"].get((0, c), st["X"]), st["Xc"].get((1, c), st["X"])],
                dma="spill")
        rsc_ln.close()
        xsc1.close()

        lsc_qkv = contextlib.ExitStack()
        lb = lambda n, s, dt: lsc_qkv.enter_context(nc.sbuf_tensor(n, s, dt))
        QR = lb("qr", [128, 16 * 1024], BF16)
        KR = lb("kr", [128, 16 * 1024], BF16)
        VT = lb("vt", [128, 16 * 1024], BF16)
        WQB = rbq("wqb", [128, NC * 1024], BF16)
        TCOS = [rbq(f"tcos{i}", [128, 512], F32) for i in range(2)]
        TSIN = [rbq(f"tsin{i}", [128, 512], F32) for i in range(2)]
        RSC1 = rbq("rsc1", [128, 512], F32)
        RSC2 = rbq("rsc2", [128, 512], F32)

        def r3(ap):
            return ap.rearrange("p (h j) -> p h j", h=NH)

        wq_cons = [tok_ln]
        tab_cons = [None, None]
        qraw_cons = [None, None]
        tok_qkv_last = tok_ln
        VTS = VT[:, 0:NC * 1024]
        tok_wQ = P.add("gpsimd", lambda e: e.dma_start(
            out=WQB.rearrange("p (c n) -> p c n", c=NC),
            in_=dr["wqkv"].rearrange("(c p) n -> p c n", p=128)[:, :, 0:1024]),
            deps=wq_cons, dma="wqkv")
        tok_wK = P.add("gpsimd", lambda e: e.dma_start(
            out=VTS.rearrange("p (c n) -> p c n", c=NC),
            in_=dr["wqkv"].rearrange("(c p) n -> p c n", p=128)[:, :, 1024:2048]),
            deps=wq_cons, dma="wqkv2")
        VWS = CAT[:, 0:NC * 1024]
        tok_wV = P.add("gpsimd", lambda e: e.dma_start(
            out=VWS.rearrange("p (c n) -> p c n", c=NC),
            in_=dr["wqkv"].rearrange("(c p) n -> p c n", p=128)[:, :, 2048:3072]),
            deps=wq_cons, dma="wqkv3")
        P.force_drain = False
        for wh in range(3):
            WSRC = (WQB, VTS, VWS)[wh]
            tok_w = (tok_wQ, tok_wK, tok_wV)[wh]
            tok_mm_last = None
            for tt in range(16):
                pt = (wh * 16 + tt) % 2
                if wh < 2:
                    tok_tc = P.add("sync", lambda e, tt=tt, pt=pt: e.dma_start(
                        out=TCOS[pt][:], in_=dr["cos_t"][tt * 128:(tt + 1) * 128, :]),
                        deps=[tab_cons[pt]], dma=f"tcos{pt}")
                    tok_ts = P.add("sync", lambda e, tt=tt, pt=pt: e.dma_start(
                        out=TSIN[pt][:], in_=dr["sin_t"][tt * 128:(tt + 1) * 128, :]),
                        deps=[tab_cons[pt]], dma=f"tsin{pt}")
                par = tt % 2
                kp = f"p{par}a"
                dmm = (PM.write_deps(kp) + PM.write_deps(f"p{par}b")
                       + [tok_w, tok_ln, tok_spill])
                tok_mm = None
                for g in range(2):
                    for c in range(NC):
                        tok_mm = P.add("tensor", lambda e, tt=tt, g=g, c=c, W=WSRC, par=par: e.matmul(
                            PS[par][:, g * 512:(g + 1) * 512],
                            XNA[:, c * T + tt * 128: c * T + (tt + 1) * 128],
                            W[:, c * 1024 + g * 512: c * 1024 + (g + 1) * 512],
                            start=(c == 0), stop=(c == NC - 1)),
                            deps=dmm if (g == 0 and c == 0) else [],
                            signal=(g == 1 and c == NC - 1))
                PM.wrote(kp, tok_mm); PM.wrote(f"p{par}b", tok_mm)
                tok_mm_last = tok_mm
                dre = PM.read_deps(kp)
                if wh == 2:
                    tok_ev = P.add("scalar", lambda e, tt=tt, par=par: e.copy(
                        VT[:, tt * 1024:(tt + 1) * 1024], PS[par][:, 0:1024]),
                        deps=dre, signal=True)
                    PM.read(kp, tok_ev); PM.read(f"p{par}b", tok_ev)
                    tok_qkv_last = tok_ev
                else:
                    DST = QR if wh == 0 else KR
                    d1 = r3(DST[:, tt * 1024:(tt + 1) * 1024])[:, :, 0:32]
                    d2 = r3(DST[:, tt * 1024:(tt + 1) * 1024])[:, :, 32:64]
                    P.add("vector", lambda e, pt=pt, par=par: e.tensor_tensor(
                        RSC1[:, :], PS[par][:, 0:512], TCOS[pt][:, :], AluOpType.mult),
                        deps=dre + [tok_tc], drain=True)
                    P.add("vector", lambda e, pt=pt, par=par: e.tensor_tensor(
                        RSC2[:, :], PS[par][:, 512:1024], TSIN[pt][:, :], AluOpType.mult),
                        deps=[tok_ts])
                    P.add("vector", lambda e, o=d1, pt=pt: e.tensor_tensor(
                        o, r3(RSC1[:, :]), r3(RSC2[:, :]), AluOpType.subtract), drain=True)
                    P.add("vector", lambda e, pt=pt, par=par: e.tensor_tensor(
                        RSC1[:, :], PS[par][:, 0:512], TSIN[pt][:, :], AluOpType.mult),
                        drain=True)
                    P.add("vector", lambda e, pt=pt, par=par: e.tensor_tensor(
                        RSC2[:, :], PS[par][:, 512:1024], TCOS[pt][:, :], AluOpType.mult))
                    tok_rp = P.add("vector", lambda e, o=d2: e.tensor_tensor(
                        o, r3(RSC1[:, :]), r3(RSC2[:, :]), AluOpType.add),
                        signal=True, drain=True)
                    PM.read(kp, tok_rp); PM.read(f"p{par}b", tok_rp)
                    tab_cons[pt] = tok_rp
                    tok_qkv_last = tok_rp
        P.force_drain = True
        rsc_qkvm.close()
        lsc_wp = contextlib.ExitStack()
        WPB = lsc_wp.enter_context(nc.sbuf_tensor(
            "wpb", [128, NC * 1024], BF16, side="right"))
        tok_wp = P.add("gpsimd", lambda e: e.dma_start(
            out=WPB.rearrange("p (c n) -> p c n", c=NC),
            in_=dr["wproj"].rearrange("(c p) n -> p c n", p=128)),
            deps=[tok_qkv_last, tab_cons[0], tab_cons[1]], dma="wqkv")
        for k in ("p0a", "p0b", "p1a", "p1b", "p2a", "p2b", "p3a", "p3b"):
            PM.read(k, tok_qkv_last)

        lsc_hd = contextlib.ExitStack()
        hb_ = lambda n, s, dt: lsc_hd.enter_context(nc.sbuf_tensor(n, s, dt))
        QEXT = [hb_(f"qext{i}", [65, T], BF16) for i in range(2)]
        KEXT = [hb_(f"kext{i}", [65, T], BF16) for i in range(2)]
        SQQ = hb_("sqq", [64, T], BF16)
        SQK = hb_("sqk", [64, T], BF16)
        ATT_E = [hb_(f"atte{i}", [128, T], BF16) for i in range(3)]
        ATT_T = hb_("attt", [128, 1024], BF16)
        DBUF = [hb_(f"dbuf{i}", [128, 128], BF16) for i in range(3)]
        MNEG = hb_("mneg", [128, 16], F32)
        SROW = hb_("srow", [128, 16], F32)
        RINVB = hb_("rinvb", [128, 4], BF16)
        CS_SB = hb_("cs_sb", [128, 16], BF16)
        KMX = hb_("kmx", [64, 1], BF16)
        S0SB = hb_("s0sb", [128, 1], F32)

        tok_q1 = [P.add("vector", lambda e, i=i: e.memset(QEXT[i][64:65, :], 1.0),
                        signal=True) for i in range(2)]
        ext_cons = [[tok_qkv_last, tok_q1[0], tok_q1[1]], [tok_qkv_last, tok_q1[0], tok_q1[1]]]
        slot_cons = [None, None, None]
        attt_cons = [None, None]
        cs_cons = [None]
        tok_bias = {}
        tok_mneg = {}
        tok_ext = {}
        tok_sq_cons = [None]

        def prepare_head(h):
            b = h % 2
            tok_tr_ev = []
            for rnd in range(2):
                dtr = (PM.write_deps("p1a") + PM.write_deps("p1b")
                       + ext_cons[b] + [tok_qkv_last])
                tok_tr = None
                for tti in range(8):
                    tt = rnd * 8 + tti
                    P.add("tensor", lambda e, tt=tt, tti=tti, h=h: e.matmul(
                        PS[1][0:64, tti * 128:(tti + 1) * 128],
                        QR[:, tt * 1024 + h * 64: tt * 1024 + h * 64 + 64],
                        C_identb[:], start=True, stop=True),
                        deps=dtr if tti == 0 else [])
                    tok_tr = P.add("tensor", lambda e, tt=tt, tti=tti, h=h: e.matmul(
                        PS[1][64:128, tti * 128:(tti + 1) * 128],
                        KR[:, tt * 1024 + h * 64: tt * 1024 + h * 64 + 64],
                        C_identb[:], start=True, stop=True, tile_position=(0, 64)),
                        deps=[], signal=(tti == 7))
                PM.wrote("p1a", tok_tr); PM.wrote("p1b", tok_tr)
                dtv = PM.read_deps("p1a") + PM.read_deps("p1b")
                tok_eq = P.add("scalar", lambda e, rnd=rnd, b=b: e.copy(
                    QEXT[b][0:64, rnd * 1024:(rnd + 1) * 1024], PS[1][0:64, 0:1024]),
                    deps=dtv, signal=True)
                tok_ek = P.add("vector", lambda e, rnd=rnd, b=b: e.tensor_copy(
                    KEXT[b][0:64, rnd * 1024:(rnd + 1) * 1024], PS[1][64:128, 0:1024]),
                    deps=dtv, signal=True)
                PM.read("p1a", tok_eq); PM.read("p1a", tok_ek)
                PM.read("p1b", tok_eq); PM.read("p1b", tok_ek)
                tok_tr_ev += [tok_eq, tok_ek]
            tok_sq = P.add("scalar", lambda e, b=b: e.activation(
                SQQ[:, :], QEXT[b][0:64, :], AF.Square),
                deps=tok_tr_ev + [tok_sq_cons[0]], signal=True)
            tok_sk = P.add("vector", lambda e, b=b: e.tensor_tensor(
                SQK[:, :], KEXT[b][0:64, :], KEXT[b][0:64, :], AluOpType.mult),
                deps=tok_tr_ev + [tok_sq_cons[0]], signal=True)
            dq2 = PM.write_deps("p3b") + [tok_sq, T_INIT]
            tok_q2 = None
            for qt in range(16):
                tok_q2 = P.add("tensor", lambda e, qt=qt: e.matmul(
                    PS[3][:, 768 + qt:769 + qt], SQQ[:, qt * 128:(qt + 1) * 128],
                    C_ones64[:], start=True, stop=True),
                    deps=dq2 if qt == 0 else [], signal=(qt == 15))
            tok_km = P.add("vector", lambda e: e.tensor_reduce(
                KMX[:, :], SQK[:, :], AX.X, AluOpType.max), deps=[tok_sk], signal=True)
            tok_kb = P.add("tensor", lambda e: e.matmul(
                PS[3][0:1, 784:785], C_ones64[:], KMX[:, :], start=True, stop=True),
                deps=[tok_km], signal=True)
            wa = WVROW[0:1, 16 + h - 1:16 + h] if h > 0 else WVROW[0:1, 31:32]
            tok_s0 = P.add("vector", lambda e, wa=wa: e.tensor_scalar(
                STATS[0:1, 0:1], PS[3][0:1, 784:785], 0.0625, wa,
                AluOpType.mult, AluOpType.add), deps=[tok_kb, t_wv], signal=True)
            tok_s0r = P.add("tensor", lambda e: e.matmul(
                PS[3][:, 785:786], C_orowf[:], STATS[0:1, 0:1], start=True, stop=True),
                deps=[tok_s0, tok_q2], signal=True)
            PM.wrote("p3b", tok_s0r)
            d3 = PM.read_deps("p3b")
            tok_s0sb = P.add("vector", lambda e: e.tensor_copy(
                S0SB[:, :], PS[3][:, 785:786]), deps=d3, signal=True)
            tok_mn = P.add("vector", lambda e: e.tensor_scalar(
                MNEG[:, :], PS[3][:, 768:784], -0.0625, S0SB[:, 0:1],
                AluOpType.mult, AluOpType.subtract), deps=[tok_s0sb], signal=True)
            PM.read("p3b", tok_mn)
            tok_mneg[h] = tok_mn
            tok_ext[h] = tok_tr_ev
            tok_sq_cons[0] = tok_q2
            if h == 0:
                tok_bias[0] = P.add("vector", lambda e: e.memset(
                    KEXT[0][64:65, :], 0.0), signal=True)

        def head_qt_loop(h, inject=None):
            b = h % 2
            cons = []

            def emit_scores(qt):
                par = qt % 2
                s3 = qt % 3
                kw = (qt + 1) * 128
                toks_exp = []
                for bi in range((kw + 1023) // 1024):
                    c0, c1 = bi * 1024, min(kw, (bi + 1) * 1024)
                    ka, kb = f"p{par}a", f"p{par}b"
                    dsc = (PM.write_deps(ka) + PM.write_deps(kb)
                           + [tok_mneg[h], tok_bias[h]]
                           + (tok_ext[h] if qt <= 1 and bi == 0 else []))
                    glist = list(range((c1 - c0 + 511) // 512))
                    if c1 == kw and len(glist) > 1:
                        glist = glist[::-1]
                    tok_sc = tok_diag = None
                    for gi, g in enumerate(glist):
                        g0 = c0 + g * 512
                        g1 = min(c1, g0 + 512)
                        tok_sc = P.add("tensor", lambda e, qt=qt, g0=g0, g1=g1, c0=c0, b=b, par=par: e.matmul(
                            PS[par][:, g0 - c0:g1 - c0],
                            QEXT[b][:, qt * 128:(qt + 1) * 128],
                            KEXT[b][:, g0:g1], start=True, stop=True),
                            deps=dsc if gi == 0 else [],
                            signal=(gi == 0 or gi == len(glist) - 1))
                        if gi == 0:
                            tok_diag = tok_sc
                    PM.wrote(ka, tok_sc); PM.wrote(kb, tok_sc)
                    toks = [tok_sc]
                    if c1 == kw:
                        dcol = (kw - 128) - c0
                        tok_mask = P.add("vector", lambda e, dcol=dcol, par=par: e.tensor_tensor(
                            PS[par][:, dcol:dcol + 128], PS[par][:, dcol:dcol + 128],
                            C_mask[:], AluOpType.add), deps=[tok_diag, T_INIT], signal=True)
                        toks = [tok_sc, tok_mask]
                    tok_exp = P.add("scalar", lambda e, qt=qt, c0=c0, c1=c1, bi=bi, par=par, s3=s3: e.activation(
                        ATT_E[s3][:, c0:c1], PS[par][:, 0:c1 - c0],
                        AF.Exp, bias=MNEG[:, qt:qt + 1], scale=0.125,
                        accum_out=SROW[:, s3 * 2 + bi:s3 * 2 + bi + 1]),
                        deps=toks + [slot_cons[s3]], signal=True)
                    PM.read(ka, tok_exp); PM.read(kb, tok_exp)
                    toks_exp.append(tok_exp)
                return toks_exp

            def emit_tail_vec(qt, toks_exp):
                par = qt % 2
                s3 = qt % 3
                kw = (qt + 1) * 128
                n_blk = (kw + 1023) // 1024
                tok_s = toks_exp[-1]
                if n_blk > 1:
                    tok_s = P.add("vector", lambda e, s3=s3: e.tensor_tensor(
                        SROW[:, s3 * 2:s3 * 2 + 1], SROW[:, s3 * 2:s3 * 2 + 1],
                        SROW[:, s3 * 2 + 1:s3 * 2 + 2],
                        AluOpType.add), deps=toks_exp, signal=True)
                tok_ri = P.add("vector", lambda e, s3=s3: e.reciprocal(
                    SROW[:, 12 + s3:13 + s3], SROW[:, s3 * 2:s3 * 2 + 1]),
                    deps=[tok_s], signal=True)
                tok_rb = P.add("vector", lambda e, s3=s3: e.tensor_scalar(
                    RINVB[:, s3:s3 + 1], SROW[:, 12 + s3:13 + s3], 1.0, None,
                    AluOpType.mult), deps=[slot_cons[s3], tok_ri], signal=True)
                tok_D = P.add("vector", lambda e, s3=s3: e.tensor_scalar(
                    DBUF[s3][:, :], C_identb[:], SROW[:, 12 + s3:13 + s3], None,
                    AluOpType.mult), deps=[T_INIT, slot_cons[s3], tok_ri], signal=True)
                return tok_rb, tok_D

            def emit_tail(qt, toks_exp, tok_rb, tok_D):
                par = qt % 2
                s3 = qt % 3
                kw = (qt + 1) * 128
                groups = list(enumerate(range(0, qt + 1, 4)))
                avc = 512 + (qt % 2) * 128
                tok_av = None

                def emit_av(gi, kt0):
                    nonlocal tok_av
                    kts = list(range(kt0, min(kt0 + 4, qt + 1)))
                    reg = gi % 2
                    dav = (PM.write_deps("p3b") if gi == 0 else []) + [cast_toks[gi]]
                    for kt in kts:
                        tok_av = P.add("tensor", lambda e, kt=kt, qt=qt, h=h, avc=avc, reg=reg: e.matmul(
                            PS[3][(h % 2) * 64:(h % 2) * 64 + 64, avc:avc + 128],
                            VT[:, kt * 1024 + h * 64: kt * 1024 + h * 64 + 64],
                            ATT_T[:, reg * 512 + (kt % 4) * 128: reg * 512 + (kt % 4 + 1) * 128],
                            start=(kt == 0), stop=(kt == qt),
                            tile_position=(0, (h % 2) * 64)),
                            deps=dav if kt == kts[0] else [],
                            signal=(kt == kts[-1]))
                    attt_cons[reg] = tok_av

                cast_toks = {}
                for gi, kt0 in groups:
                    kts = list(range(kt0, min(kt0 + 4, qt + 1)))
                    reg = gi % 2
                    pk = f"p2{'ab'[reg]}"
                    dT = PM.write_deps(pk) + [tok_D] + toks_exp
                    tok_T = None
                    tok_mm2 = None
                    for kt in kts:
                        tok_T = P.add("tensor", lambda e, kt=kt, s3=s3, reg=reg: e.matmul(
                            PS[2][:, reg * 512 + (kt % 4) * 128: reg * 512 + (kt % 4 + 1) * 128],
                            ATT_E[s3][:, kt * 128:(kt + 1) * 128],
                            DBUF[s3][:, :], start=True, stop=True),
                            deps=dT if kt == kt0 else [], signal=(kt == kts[-1]))
                        tok_mm2 = P.add("tensor", lambda e, kt=kt, s3=s3: e.matmul(
                            PS[3][:, kt:kt + 1],
                            ATT_E[s3][:, kt * 128:(kt + 1) * 128],
                            RINVB[:, s3:s3 + 1],
                            start=(qt == 0 and kt == 0),
                            stop=(qt == 15 and kt == 15),
                            skip_group_check=True),
                            deps=(PM.write_deps("p3a") + [tok_rb]) if (qt == 0 and kt == 0) else [],
                            signal=(qt == 15 and kt == 15))
                    PM.wrote(pk, tok_T)
                    if qt == 15 and kts[-1] == 15:
                        PM.wrote("p3a", tok_mm2)
                    dTe = PM.read_deps(pk) + [attt_cons[reg]]
                    if gi % 2 == 0:
                        tok_Te = P.add("vector", lambda e, n=len(kts), reg=reg: e.tensor_copy(
                            ATT_T[:, reg * 512: reg * 512 + n * 128],
                            PS[2][:, reg * 512: reg * 512 + n * 128]), deps=dTe, signal=True)
                    else:
                        tok_Te = P.add("vector", lambda e, n=len(kts), reg=reg: e.tensor_copy(
                            ATT_T[:, reg * 512: reg * 512 + n * 128],
                            PS[2][:, reg * 512: reg * 512 + n * 128]), deps=dTe, signal=True)
                    PM.read(pk, tok_Te)
                    cast_toks[gi] = tok_Te
                    if gi >= 1:
                        emit_av(gi - 1, groups[gi - 1][1])
                emit_av(groups[-1][0], groups[-1][1])
                PM.wrote("p3b", tok_av)
                slot_cons[s3] = tok_av
                cons.append(tok_av)
                if qt % 2 == 1:
                    dce = PM.read_deps("p3b")
                    tok_ce = P.add("vector", lambda e, qt=qt, h=h: e.tensor_copy(
                        CAT[(h % 2) * 64:(h % 2) * 64 + 64,
                            (h // 2) * T + (qt - 1) * 128:(h // 2) * T + (qt + 1) * 128],
                        PS[3][(h % 2) * 64:(h % 2) * 64 + 64, 512:768]),
                        deps=dce, signal=True)
                    PM.read("p3b", tok_ce)
                    cons.append(tok_ce)

            pend = []
            for qt in range(16):
                if pend and pend[0][0] == qt - 2:
                    item = pend[0]
                    pend[0] = item + emit_tail_vec(item[0], item[1])
                toks = emit_scores(qt)
                if len(pend) >= 2:
                    emit_tail(*pend.pop(0))
                pend.append((qt, toks))
            if inject is not None:
                inject()
            for item in pend:
                item = item + emit_tail_vec(item[0], item[1])
                emit_tail(*item)
            ext_cons[b] = cons
            return cons

        def head_cs(h):
            d3 = PM.read_deps("p3a")
            tok_cs = P.add("vector", lambda e: e.tensor_copy(
                CS_SB[:, :], PS[3][:, 0:16]), deps=d3 + [cs_cons[0]], signal=True)
            PM.read("p3a", tok_cs)
            tok_b = None
            tok_mt_all = None
            for rnd in range(2):
                dmt = PM.write_deps("p0a") + PM.write_deps("p0b") + [tok_cs, T_INIT]
                tok_mt = None
                for kti in range(8):
                    kt = rnd * 8 + kti
                    tok_mt = P.add("tensor", lambda e, kt=kt, kti=kti: e.matmul(
                        PS[0][0:1, kti * 128:(kti + 1) * 128],
                        CS_SB[:, kt:kt + 1], C_identb[:], start=True, stop=True),
                        deps=dmt if kti == 0 else [], signal=(kti == 7))
                PM.wrote("p0a", tok_mt); PM.wrote("p0b", tok_mt)
                tok_mt_all = tok_mt
                dbe = PM.read_deps("p0a")
                tok_b = P.add("scalar", lambda e, rnd=rnd, h=h: e.activation(
                    KEXT[(h + 1) % 2][64:65, rnd * 1024:(rnd + 1) * 1024],
                    PS[0][0:1, 0:1024], AF.Copy,
                    scale=WVROW[0:1, h:h + 1]), deps=dbe + ext_cons[(h + 1) % 2],
                    signal=True)
                PM.read("p0a", tok_b); PM.read("p0b", tok_b)
            cs_cons[0] = tok_mt_all
            tok_bias[h + 1] = tok_b

        P.force_drain = False
        prepare_head(0)
        for h in range(NH):
            head_qt_loop(h, inject=(
                (lambda hh=h: prepare_head(hh + 1)) if h + 1 < NH else None))
            if h + 1 < NH:
                head_cs(h)
        P.force_drain = True
        lsc_hd.close()
        lsc_qkv.close()

        lsc_pr = contextlib.ExitStack()
        X2 = lsc_pr.enter_context(nc.sbuf_tensor("X2", [128, NC * T], F32))
        tok_rstc = {}
        for cdb in range(NC):
            tok_rstc[cdb] = P.add("sync", lambda e, cdb=cdb: e.dma_start(
                out=X2[:, cdb * T:(cdb + 1) * T],
                in_=xspill[cdb * 128:(cdb + 1) * 128, :]),
                deps=([tok_spill] + snapshot()) if cdb == 0 else [], dma="spill")
        tok_pr_last = None
        for db in range(NC):
            pkeys = [["p0a", "p0b"], ["p1a", "p1b"]][db % 2]
            pidx = db % 2
            for tc in range(4):
                dpm = []
                for pk in pkeys:
                    dpm += PM.write_deps(pk)
                dpm += [tok_wp]
                tok_pm = None
                for c in range(NC):
                    tok_pm = P.add("tensor", lambda e, db=db, tc=tc, c=c, pidx=pidx: e.matmul(
                        PS[pidx][:, (tc % 2) * 512:(tc % 2 + 1) * 512],
                        WPB[:, c * 1024 + db * 128: c * 1024 + (db + 1) * 128],
                        CAT[:, c * T + tc * 512: c * T + (tc + 1) * 512],
                        start=(c == 0), stop=(c == NC - 1)),
                        deps=dpm if c == 0 else [], signal=(c == NC - 1))
                for pk in pkeys:
                    PM.wrote(pk, tok_pm)
                dpe = []
                for pk in pkeys:
                    dpe += PM.read_deps(pk)
                tok_pe = P.add("vector", lambda e, db=db, tc=tc, pidx=pidx: e.tensor_tensor(
                    X2[:, db * T + tc * 512: db * T + (tc + 1) * 512],
                    X2[:, db * T + tc * 512: db * T + (tc + 1) * 512],
                    PS[pidx][:, (tc % 2) * 512:(tc % 2 + 1) * 512], AluOpType.add),
                    deps=dpe + [tok_rstc[db]], signal=True)
                for pk in pkeys:
                    PM.read(pk, tok_pe)
                if tc == 1:
                    st["Xc"][(0, db)] = tok_pe
                elif tc == 3:
                    st["Xc"][(1, db)] = tok_pe
                tok_pr_last = tok_pe
        st["Xh"] = {0: tok_pr_last, 1: tok_pr_last}
        lsc_wp.close()
        rsc_cat.close()
        for k in ("p0", "p1", "p2", "p3"):
            PM.read(k, tok_pr_last)
        st["X"] = tok_pr_last
        return X2, lsc_pr

    def snapshot():
        toks = [("x", s, P.ct[s], 0) for s in ("sT", "sV", "sS", "sG") if P.ct[s] > 0]
        toks += [("x", s, P.ct[s], 0) for s in sorted(P.dma_sems) if P.ct.get(s, 0) > 0]
        return toks

    with nc.Block() as block:
        def open_ffn_scope(tag):
            sc = contextlib.ExitStack()
            f = lambda n, s, dt: sc.enter_context(nc.sbuf_tensor(n + tag, s, dt))
            t = dict(
                XNt=f("XN", [128, NC * 1024], BF16),
                warena=f("warena", [128, 8192], BF16),
                s_sb=f("s_sb", [128, 16 * 1024], BF16),
                acts=f("acts", [128, 16 * 1024], BF16),
                xsq=f("xsq", [128, 2048], F32R),
                scr=f("scr", [128, 1024], F32),
                silu_sb=f("silu_sb", [128, 4096], BF16),
                sgb=f("sgb", [128, 2048], BF16),
                cwb=f("cwb", [128, 1024], BF16),
            )
            return sc, t

        if "ffnA" in stages:
            sc1, tt = open_ffn_scope("1")
            ffn_call(X, **tt)
            sc1.close()
        XOUT = X
        _endscopes = []
        if "attn" in stages:
            XOUT, _prscope = attention(snapshot())
            _endscopes.append(_prscope)
        if "ffnC" in stages:
            sc2, tt = open_ffn_scope("2")
            ffn_call(XOUT, guard=snapshot(), **tt)
            sc2.close()
        tok_out = P.add("sync", lambda e, XO=XOUT: e.dma_start(
            out=yout.rearrange("(c p) t -> p c t", p=128),
            in_=XO.rearrange("p (c t) -> p c t", c=NC)),
            deps=[st["X"]] + snapshot(), dma="out")
        P.add("sync", lambda e: None, deps=[tok_out])
        for s in sorted(P.dma_sems):
            sems[s] = est.enter_context(nc.semaphore(s))
        P.emit(block, sems)
        for _s in reversed(_endscopes):
            _s.close()
        xsc1.close()
    est.close()
    return nc


def kernel(**inputs):
    in_maps = _host_prep(inputs)
    nc = build_program()
    res = run_bass_kernel_spmd(nc, in_maps, core_ids=list(range(8)))
    return np.stack([np.ascontiguousarray(r["yout"].T) for r in res.results], 0)
